# version 06
# baseline (speedup 1.0000x reference)
import numpy as np
import concourse.bass as bass
import concourse.mybir as mybir
from concourse.bass_utils import run_bass_kernel_spmd
from contextlib import ExitStack

F32 = mybir.dt.float32
BF16 = mybir.dt.bfloat16
I32 = mybir.dt.int32
AF = mybir.ActivationFunctionType
ALU = mybir.AluOpType
PI = float(np.pi)

D, T, NTC, NKC, NH = 2048, 1024, 8, 16, 16
INTER, NIC, BH = 5632, 44, 64
SEQ, NRANK = 4096, 4
GROUPS = [[0, 1, 2, 3], [4, 5, 6, 7]]
ISQ = 1.0 / float(np.sqrt(128.0))
EXPSHIFT = -20.0


class Sem:
    def __init__(self, st, nc, name):
        self.h = st.enter_context(nc.semaphore(name))
        self.n = 0

    def bump(self, instr, k=1):
        instr.then_inc(self.h, k)
        self.n += k


class Stream:
    def __init__(self, st, nc, name, nsem=4, depth=None, gate=None):
        self.sems = [st.enter_context(nc.semaphore(f"{name}c{j}")) for j in range(nsem)]
        self.freesem = st.enter_context(nc.semaphore(f"{name}f"))
        self.nsem, self.depth, self.gate, self.ni = nsem, depth, gate, 0

    def issue(self, e, out, in_):
        i = self.ni
        self.ni += 1
        if self.depth is not None and i >= self.depth:
            if self.gate == "free":
                e.wait_ge(self.freesem, i - self.depth + 1)
            elif self.gate == "self":
                self.wait(e, i - self.depth)
        e.dma_start(out=out, in_=in_).then_inc(self.sems[i % self.nsem], 16)
        return i

    def wait(self, e, i):
        e.wait_ge(self.sems[i % self.nsem], 16 * (i // self.nsem + 1))

    def join(self, e):
        for j in range(min(self.nsem, self.ni)):
            i = max(k for k in range(self.ni) if k % self.nsem == j)
            self.wait(e, i)


def _mpan(W):
    Kd, Md = W.shape
    kc, mc = Kd // 128, Md // 128
    return np.ascontiguousarray(
        W.reshape(kc, 128, mc, 128).transpose(1, 2, 0, 3).reshape(128, mc * kc * 128))


def _kpan(W):
    Kd, Md = W.shape
    kc = Kd // 128
    return np.ascontiguousarray(
        W.reshape(kc, 128, Md).transpose(1, 0, 2).reshape(128, kc * Md))


def build_nc():
    nc = bass.Bass("TRN2", target_bir_lowering=False)

    xe = nc.dram_tensor("x", [T, D], F32, kind="ExternalInput").ap()
    oe = nc.dram_tensor("out", [T, D], F32, kind="ExternalOutput").ap()
    wq_d = nc.dram_tensor("wq", [128, NKC * D], F32, kind="ExternalInput").ap()
    wk_d = nc.dram_tensor("wk", [128, NKC * D], F32, kind="ExternalInput").ap()
    wv_d = nc.dram_tensor("wv", [128, NKC * D], F32, kind="ExternalInput").ap()
    wo_d = nc.dram_tensor("wo", [128, NKC * D], F32, kind="ExternalInput").ap()
    gate_d = nc.dram_tensor("gate", [128, NIC * D], F32, kind="ExternalInput").ap()
    up_d = nc.dram_tensor("up", [128, NIC * D], F32, kind="ExternalInput").ap()
    down_d = nc.dram_tensor("down", [128, NKC * INTER], F32, kind="ExternalInput").ap()
    bq_d = nc.dram_tensor("bq", [128, NKC * D], F32, kind="ExternalInput").ap()
    bk_d = nc.dram_tensor("bk", [128, NKC * D], F32, kind="ExternalInput").ap()
    bv_d = nc.dram_tensor("bv", [128, NKC * D], F32, kind="ExternalInput").ap()
    bp_d = nc.dram_tensor("bp", [128, NKC * D], F32, kind="ExternalInput").ap()
    gs_d = nc.dram_tensor("gs", [128, NIC], F32, kind="ExternalInput").ap()
    us_d = nc.dram_tensor("us", [128, NIC], F32, kind="ExternalInput").ap()
    ds_d = nc.dram_tensor("ds", [128, NKC], F32, kind="ExternalInput").ap()
    bpb_d = nc.dram_tensor("bpb", [128, D], F32, kind="ExternalInput").ap()
    poff_d = nc.dram_tensor("poff", [128, 1], F32, kind="ExternalInput").ap()

    xd = nc.dram_tensor("xd", [T, D], F32).ap()
    kgi = nc.dram_tensor("kgi", [D, T], BF16).ap()
    kgo = nc.dram_tensor("kgo", [NRANK * D, T], BF16).ap()
    vgi = nc.dram_tensor("vgi", [NH * T, 128], BF16).ap()
    vgo = nc.dram_tensor("vgo", [NRANK * NH * T, 128], BF16).ap()
    vki = nc.dram_tensor("vki", [32, BH * 33], F32).ap()
    vko = nc.dram_tensor("vko", [32, BH * 33], F32).ap()

    xrows = xe.rearrange("(a p) d -> a p d", p=128)
    orows = oe.rearrange("(a p) d -> a p d", p=128)
    xdrows = xd.rearrange("(a p) d -> a p d", p=128)

    st = ExitStack()
    with st:
        def SL(name, shape, dt=F32, stk=st):
            return stk.enter_context(nc.sbuf_tensor(name, shape, dt, side="left"))

        def SR(name, shape, dt=F32, stk=st):
            return stk.enter_context(nc.sbuf_tensor(name, shape, dt, side="right"))

        mring = [SL(f"mr{i}", [128, D], BF16) for i in range(3)]
        tt1 = SL("tt1", [128, T])
        tt2 = SL("tt2", [128, T])
        consts = SL("consts", [128, 2])
        ident = SL("ident", [128, 128], BF16)
        shiftm = SL("shiftm", [128, 128], BF16)
        gs_t = SL("gs_t", [128, NIC])
        us_t = SL("us_t", [128, NIC])
        ds_t = SL("ds_t", [128, NKC])
        poff_t = SL("poff_t", [128, 1])
        bnacc = SL("bnacc", [128, 24])
        bnmv = SL("bnmv", [128, 2])
        rstd = SL("rstd", [128, 1])
        recl = SL("recl", [128, 1])
        ones33 = SL("ones33", [33, 32], BF16)

        pA = st.enter_context(nc.psum_tensor("pA", [128, 1024], F32))
        pB = st.enter_context(nc.psum_tensor("pB", [128, 1024], F32))
        pC = st.enter_context(nc.psum_tensor("pC", [128, 1024], F32))
        pF = st.enter_context(nc.psum_tensor("pF", [128, 512], F32))
        pE = st.enter_context(nc.psum_tensor("pE", [128, 1024], BF16))

        sT = Sem(st, nc, "sT")
        sV = Sem(st, nc, "sV")
        sS = Sem(st, nc, "sS")
        sG = Sem(st, nc, "sG")
        ccK = Sem(st, nc, "ccK")
        ccV = Sem(st, nc, "ccV")
        ccR = Sem(st, nc, "ccR")
        kvdone = Sem(st, nc, "kvdone")
        kvf = Sem(st, nc, "kvf")
        kresdone = Sem(st, nc, "kresd")
        ydone = Sem(st, nc, "ydone")

        mst = Stream(st, nc, "mst", nsem=4, depth=3, gate="free")
        kst = Stream(st, nc, "kst", nsem=4)
        dstm = Stream(st, nc, "dstm", nsem=4, depth=3, gate="free")
        yst = Stream(st, nc, "yst", nsem=8)
        cst = Stream(st, nc, "cst", nsem=4, depth=2, gate="self")
        xst = Stream(st, nc, "xst", nsem=4, depth=2, gate="free")
        ost = Stream(st, nc, "ost", nsem=4)
        kvst = Stream(st, nc, "kvst", nsem=8)

        block = st.enter_context(nc.Block())

        def g_setup(e):
            def bar(instr, wait=True):
                sG.bump(instr)
                if wait:
                    e.wait_ge(sG.h, sG.n)
            bar(e.memset(ident[:, :], 0.0))
            bar(e.affine_select(out=ident[:, :], in_=ident[:, :],
                                compare_op=ALU.not_equal, fill=1.0, base=0,
                                pattern=[[-1, 128]], channel_multiplier=1), False)
            bar(e.memset(shiftm[:, :], 0.0))
            bar(e.affine_select(out=shiftm[:, :], in_=shiftm[:, :],
                                compare_op=ALU.not_equal, fill=1.0, base=1,
                                pattern=[[-1, 128]], channel_multiplier=1))
            bar(e.affine_select(out=shiftm[:, :], in_=shiftm[:, :],
                                compare_op=ALU.not_equal, fill=1.0, base=-127,
                                pattern=[[-1, 128]], channel_multiplier=1), False)
            bar(e.memset(consts[:, 0:1], 1e-5), False)
            bar(e.memset(consts[:, 1:2], EXPSHIFT), False)
            bar(e.memset(ones33[:, :], 0.0))
            bar(e.memset(ones33[32:33, :], 1.0), False)
            bar(e.iota(tt1[:, :], [[1, T]], channel_multiplier=0,
                       allow_small_or_imprecise_dtypes=True), False)
            bar(e.iota(tt2[:, 0:1], [[1, 1]], channel_multiplier=1,
                       allow_small_or_imprecise_dtypes=True), False)
        block.gpsimd(g_setup)

        def y_small(e):
            for dv, sv_ in ((gs_t, gs_d), (us_t, us_d), (ds_t, ds_d), (poff_t, poff_d)):
                yst.issue(e, dv[:, :], sv_)
            yst.join(e)
            e.sem_inc(ydone.h, 1)
        block.sync(y_small)
        ydone.n += 1

        def v_invf(e):
            e.wait_ge(sG.h, sG.n)
            e.wait_ge(ydone.h, ydone.n)
            sV.bump(e.tensor_scalar(out=tt2[:, 1:2], in0=tt2[:, 0:1], scalar1=63.5,
                                    scalar2=None, op0=ALU.is_gt))
            e.wait_ge(sV.h, sV.n)
            sV.bump(e.tensor_scalar(out=tt2[:, 1:2], in0=tt2[:, 1:2], scalar1=64.0,
                                    scalar2=None, op0=ALU.mult))
            e.wait_ge(sV.h, sV.n)
            sV.bump(e.tensor_tensor(out=tt2[:, 0:1], in0=tt2[:, 0:1], in1=tt2[:, 1:2],
                                    op=ALU.subtract))
        block.vector(v_invf)

        def s_invf(e):
            e.wait_ge(sV.h, sV.n)
            sS.bump(e.activation(out=tt2[:, 0:1], in_=tt2[:, 0:1], func=AF.Exp,
                                 scale=-float(np.log(10000.0) / 64.0)))
        block.scalar(s_invf)

        def ln_phase(src_rows, fmA, lnb, lnbb):
            trsnap = [0, 0]
            for tc in range(NTC):
                def y_ld(e, t=tc):
                    if t == 0 and ost.ni:
                        ost.join(e)
                    xst.issue(e, lnb[t % 2][:, :], src_rows[t])
                block.sync(y_ld)
                li = xst.ni - 1

                def v_stats(e, t=tc, i=li):
                    xst.wait(e, i)
                    for j in range(4):
                        sV.bump(e.bn_stats(bnacc[:, 6 * j:6 * j + 6],
                                           lnb[t % 2][:, 512 * j:512 * (j + 1)]))
                block.vector(v_stats)

                def v_aggr(e):
                    e.wait_ge(sV.h, sV.n)
                    sV.bump(e.bn_aggr(bnmv[:, :], bnacc[:, :]))
                block.vector(v_aggr)

                def s_rstd(e):
                    e.wait_ge(sV.h, sV.n)
                    sS.bump(e.activation(out=rstd[:, :], in_=bnmv[:, 1:2], func=AF.Ln,
                                         bias=consts[:, 0:1], scale=1.0))
                    e.wait_ge(sS.h, sS.n)
                    sS.bump(e.activation(out=rstd[:, :], in_=rstd[:, :],
                                         func=AF.Exp, scale=-0.5))
                block.scalar(s_rstd)

                def v_norm(e, t=tc, wtr=trsnap[tc % 2]):
                    e.wait_ge(sS.h, sS.n)
                    if wtr:
                        e.wait_ge(sT.h, wtr)
                    sV.bump(e.tensor_scalar(
                        out=lnbb[t % 2][:, :], in0=lnb[t % 2][:, :],
                        scalar1=bnmv[:, 0:1], scalar2=rstd[:, 0:1],
                        op0=ALU.subtract, op1=ALU.mult))
                    e.wait_ge(sV.h, sV.n)
                    e.sem_inc(xst.freesem, 1)
                block.vector(v_norm)
                vnorm = sV.n

                for half in range(2):
                    def t_tr(e, t=tc, hf=half, w=max(vnorm, sV.n)):
                        e.wait_ge(sV.h, w)
                        for j in range(8):
                            dc = hf * 8 + j
                            sT.bump(e.transpose(pE[:, 128 * j:128 * (j + 1)],
                                                lnbb[t % 2][:, 128 * dc:128 * (dc + 1)],
                                                ident[:, :]))
                    block.tensor(t_tr)

                    def v_ev(e, t=tc, hf=half, w=sT.n):
                        e.wait_ge(sT.h, w)
                        dv = fmA[:, :].rearrange("p (dc q) -> p dc q", q=T)[
                            :, hf * 8:hf * 8 + 8, t * 128:t * 128 + 128]
                        sV.bump(e.tensor_copy(out=dv, in_=pE[:, :]))
                    block.vector(v_ev)
                trsnap[tc % 2] = sT.n

        def resid_tm(src_rows, dstrows, xinb, xoutb, emit_mm, emit_evac,
                     gate_t=0, gate_v=0):
            oidx = []
            for tc in range(NTC):
                def y_ld(e, t=tc):
                    if t == 0:
                        if gate_t:
                            e.wait_ge(sT.h, gate_t)
                        if gate_v:
                            e.wait_ge(sV.h, gate_v)
                    xst.issue(e, xinb[t % 2][:, :], src_rows[t])
                block.sync(y_ld)
                li = xst.ni - 1

                def t_mm(e, t=tc, w=sV.n):
                    if t >= 1:
                        e.wait_ge(sV.h, w)
                    emit_mm(e, t)
                block.tensor(t_mm)

                def v_ev(e, t=tc, i=li, w=sT.n,
                         wo_=(oidx[tc - 2] if tc >= 2 else None)):
                    e.wait_ge(sT.h, w)
                    xst.wait(e, i)
                    if wo_ is not None:
                        ost.wait(e, wo_)
                    emit_evac(e, t)
                    e.wait_ge(sV.h, sV.n)
                    e.sem_inc(xst.freesem, 1)
                block.vector(v_ev)

                def y_stv(e, t=tc, w=sV.n):
                    e.wait_ge(sV.h, w)
                    oidx.append(ost.issue(e, dstrows[t], xoutb[t % 2][:, :]))
                block.sync(y_stv)

        with ExitStack() as ph:
            qall = SL("qall", [128, NH * T], BF16, ph)
            cfm = SL("cfm", [128, NKC * T], BF16, ph)
            with ExitStack() as rA:
                fmA = SR("fmA0", [128, NKC * T], BF16, rA)
                tabs = SR("tabs", [128, 4 * T], F32, rA)
                krow = [SR(f"krw{i}", [128, T], BF16, rA) for i in range(2)]

                with ExitStack() as rB:
                    ti32 = SR("ti32", [128, T], I32, rB)
                    lnb = [SR(f"lnb0{i}", [128, D], F32, rB) for i in range(2)]
                    lnbb = [SR(f"lnbb0{i}", [128, D], BF16, rB) for i in range(2)]

                    def v_tab(e):
                        e.wait_ge(sS.h, sS.n)

                        def bar(instr):
                            sV.bump(instr)
                            e.wait_ge(sV.h, sV.n)
                        bar(e.tensor_scalar(out=tt1[:, :], in0=tt1[:, :],
                                            scalar1=poff_t[:, 0:1], scalar2=tt2[:, 0:1],
                                            op0=ALU.add, op1=ALU.mult))
                        for half, shift in ((0, PI / 2.0), (1, 0.0)):
                            dv = tabs[:, half * 2 * T:half * 2 * T + T]
                            bar(e.tensor_scalar(out=dv, in0=tt1[:, :], scalar1=shift,
                                                scalar2=1.0 / (2 * PI),
                                                op0=ALU.add, op1=ALU.mult))
                            bar(e.tensor_copy(out=ti32[:, :], in_=dv))
                            bar(e.tensor_copy(out=tt2[:, :], in_=ti32[:, :]))
                            bar(e.tensor_tensor(out=dv, in0=dv, in1=tt2[:, :],
                                                op=ALU.subtract))
                            bar(e.tensor_scalar(out=tt2[:, :], in0=dv, scalar1=0.5,
                                                scalar2=None, op0=ALU.is_gt))
                            bar(e.tensor_tensor(out=dv, in0=dv, in1=tt2[:, :],
                                                op=ALU.subtract))
                            bar(e.tensor_scalar(out=tt2[:, :], in0=dv, scalar1=-0.5,
                                                scalar2=None, op0=ALU.is_lt))
                            bar(e.tensor_tensor(out=dv, in0=dv, in1=tt2[:, :], op=ALU.add))
                            bar(e.tensor_scalar(out=dv, in0=dv, scalar1=2 * PI,
                                                scalar2=None, op0=ALU.mult))
                    block.vector(v_tab)

                    def s_tab(e):
                        e.wait_ge(sV.h, sV.n)
                        sS.bump(e.activation(out=tabs[:, 0:T], in_=tabs[:, 0:T],
                                             func=AF.Sin))
                        sS.bump(e.activation(out=tabs[:, 2 * T:3 * T],
                                             in_=tabs[:, 2 * T:3 * T], func=AF.Sin))
                    block.scalar(s_tab)

                    def v_tabq(e):
                        e.wait_ge(sS.h, sS.n)
                        e.tensor_scalar(out=tabs[:, T:2 * T], in0=tabs[:, 0:T],
                                        scalar1=ISQ, scalar2=None, op0=ALU.mult)
                        sV.bump(e.tensor_scalar(out=tabs[:, 3 * T:4 * T],
                                                in0=tabs[:, 2 * T:3 * T],
                                                scalar1=ISQ, scalar2=None,
                                                op0=ALU.mult))
                    block.vector(v_tabq)

                    ln_phase(xrows, fmA, lnb, lnbb)
                fm_ready = sV.n
                postln_T = sT.n
                cosK, cosQ = tabs[:, 0:T], tabs[:, T:2 * T]
                sinK, sinQ = tabs[:, 2 * T:3 * T], tabs[:, 3 * T:4 * T]

                with ExitStack() as rK:
                    kres = SR("kres", [128, NKC * D], BF16, rK)

                    def g_wk(e):
                        for m in range(NKC):
                            mst.issue(e, mring[mst.ni % 3][:, :],
                                      wk_d[:, D * m:D * (m + 1)])
                    block.gpsimd(g_wk)
                    wk_i = list(range(0, NKC))

                    def g_wv(e, wt=None, wv=None):
                        e.wait_ge(sT.h, postln_T)
                        e.wait_ge(sV.h, fm_ready)
                        for k in range(NKC):
                            kst.issue(e, kres[:, D * k:D * (k + 1)],
                                      wv_d[:, D * k:D * (k + 1)])
                        kst.join(e)
                        e.sem_inc(kresdone.h, 1)
                    block.gpsimd(g_wv)
                    kresdone.n += 1
                    kres_wv = kresdone.n

                    def qk_proj(widx, cosT, sinT, out_cb, contrib, join_s, join_v):
                        ev_snap = [0] * NH
                        rope_snap = [0] * NH
                        ci_snap = [None] * NH
                        for mc in range(NH):
                            pj = pA if mc % 2 == 0 else pB

                            def t_mm(e, m=mc, ps=pj, wi=widx[mc]):
                                mst.wait(e, wi)
                                if m < 2:
                                    e.wait_ge(sV.h, fm_ready)
                                    if join_s:
                                        e.wait_ge(sS.h, join_s)
                                else:
                                    e.wait_ge(sS.h, ev_snap[m - 2])
                                pan = mring[wi % 3]
                                for n in range(2):
                                    for kc in range(NKC):
                                        mm = e.matmul(
                                            ps[:, 512 * n:512 * (n + 1)],
                                            pan[:, 128 * kc:128 * (kc + 1)],
                                            fmA[:, T * kc + 512 * n:T * kc + 512 * (n + 1)],
                                            start=(kc == 0), stop=(kc == NKC - 1))
                                        if kc == NKC - 1:
                                            sT.bump(mm)
                            block.tensor(t_mm)

                            def s_ev(e, m=mc, ps=pj, w=sT.n):
                                e.wait_ge(sT.h, w)
                                e.nop().then_inc(mst.freesem, 1)
                                if m >= 2:
                                    e.wait_ge(sV.h, rope_snap[m - 2])
                                    if contrib:
                                        cst.wait(e, ci_snap[m - 2])
                                sS.bump(e.activation(out=krow[m % 2][:, :],
                                                     in_=ps[:, :],
                                                     func=AF.Copy, scale=1.0))
                            block.scalar(s_ev)
                            ev_snap[mc] = sS.n

                            def t_roll(e, m=mc, w=sS.n):
                                e.wait_ge(sS.h, w)
                                if m >= 1:
                                    e.wait_ge(sV.h, rope_snap[m - 1])
                                elif join_v:
                                    e.wait_ge(sV.h, join_v)
                                for n in range(2):
                                    sT.bump(e.matmul(pC[:, 512 * n:512 * (n + 1)],
                                                     shiftm[:, :],
                                                     krow[m % 2][:, 512 * n:512 * (n + 1)],
                                                     start=True, stop=True))
                            block.tensor(t_roll)

                            def v_rope(e, m=mc, w=sT.n):
                                e.wait_ge(sT.h, w)
                                sV.bump(e.tensor_tensor(out=tt1[:, :],
                                                        in0=krow[m % 2][:, :],
                                                        in1=cosT, op=ALU.mult))
                                sV.bump(e.tensor_tensor(out=tt2[:, :], in0=pC[:, :],
                                                        in1=sinT, op=ALU.mult))
                                e.wait_ge(sV.h, sV.n)
                                sV.bump(out_cb(e, m))
                            block.vector(v_rope)
                            rope_snap[mc] = sV.n

                            if contrib:
                                def y_c(e, m=mc, w=sV.n):
                                    e.wait_ge(sV.h, w)
                                    cst.issue(
                                        e,
                                        kgi.rearrange("(mc p) t -> mc p t", p=128)[m],
                                        krow[m % 2][:, :])
                                block.sync(y_c)
                                ci_snap[mc] = cst.ni - 1
                        return rope_snap

                    def k_out(e, m):
                        return e.tensor_tensor(out=krow[m % 2][:, :], in0=tt1[:, :],
                                               in1=tt2[:, :], op=ALU.add)
                    qk_proj(wk_i, cosK, sinK, k_out, contrib=True,
                            join_s=0, join_v=0)
                    kc_last = cst.ni - 1

                    def g_ccK(e):
                        cst.join(e)
                        e.collective_compute(
                            "AllGather", ALU.bypass, replica_groups=GROUPS,
                            ins=[kgi], outs=[kgo]).then_inc(ccK.h, 1)
                    block.gpsimd(g_ccK)
                    ccK.n += 1

                    vev_snap = [0] * NTC
                    vci = [None] * NTC
                    for tc in range(NTC):
                        def t_vmm(e, t=tc):
                            e.wait_ge(kresdone.h, kres_wv)
                            if t == 0:
                                e.wait_ge(sV.h, fm_ready)
                                e.wait_ge(sS.h, sS.n)
                            else:
                                e.wait_ge(sS.h, vev_snap[t - 1])
                            for n in range(4):
                                ps = pA if n < 2 else pB
                                psl = ps[:, 512 * (n % 2):512 * (n % 2 + 1)]
                                for kc in range(NKC):
                                    mm = e.matmul(
                                        psl,
                                        fmA[:, T * kc + 128 * t:T * kc + 128 * (t + 1)],
                                        kres[:, D * kc + 512 * n:D * kc + 512 * (n + 1)],
                                        start=(kc == 0), stop=(kc == NKC - 1))
                                    if kc == NKC - 1:
                                        sT.bump(mm)
                        block.tensor(t_vmm)

                        def s_vev(e, t=tc, w=sT.n):
                            e.wait_ge(sT.h, w)
                            if t == 0:
                                cst.wait(e, kc_last - 1)
                                cst.wait(e, kc_last)
                            else:
                                cst.wait(e, vci[t - 1][0])
                                cst.wait(e, vci[t - 1][1])
                            sS.bump(e.activation(out=krow[0][:, :], in_=pA[:, :],
                                                 func=AF.Copy, scale=1.0))
                            sS.bump(e.activation(out=krow[1][:, :], in_=pB[:, :],
                                                 func=AF.Copy, scale=1.0))
                        block.scalar(s_vev)
                        vev_snap[tc] = sS.n

                        def y_vc(e, t=tc, w=sS.n):
                            e.wait_ge(sS.h, w)
                            a = cst.issue(
                                e,
                                vgi.rearrange("(h j p) d -> j p h d",
                                              h=NH, p=128)[t][:, 0:8, :],
                                krow[0][:, :].rearrange("p (h d) -> p h d", d=128))
                            b = cst.issue(
                                e,
                                vgi.rearrange("(h j p) d -> j p h d",
                                              h=NH, p=128)[t][:, 8:16, :],
                                krow[1][:, :].rearrange("p (h d) -> p h d", d=128))
                            vci[t] = (a, b)
                        block.sync(y_vc)

                    def g_ccV(e):
                        cst.join(e)
                        e.collective_compute(
                            "AllGather", ALU.bypass, replica_groups=GROUPS,
                            ins=[vgi], outs=[vgo]).then_inc(ccV.h, 1)
                    block.gpsimd(g_ccV)
                    ccV.n += 1
                    vproj_S = sS.n
                    vc_last = cst.ni - 1

                    def g_wq(e):
                        for m in range(NKC):
                            mst.issue(e, mring[mst.ni % 3][:, :],
                                      wq_d[:, D * m:D * (m + 1)])
                    block.gpsimd(g_wq)
                    wq_i = list(range(NKC, 2 * NKC))

                    def q_out(e, m):
                        return e.tensor_tensor(out=qall[:, T * m:T * (m + 1)],
                                               in0=tt1[:, :], in1=tt2[:, :],
                                               op=ALU.add)
                    qvj = sS.n

                    def s_qgate(e):
                        cst.wait(e, vc_last - 1)
                        cst.wait(e, vc_last)
                        sS.bump(e.nop())
                    block.scalar(s_qgate)
                    qk_proj(wq_i, cosQ, sinQ, q_out, contrib=False,
                            join_s=vproj_S, join_v=sV.n)
                    q_ready = sV.n
                    q_sS = sS.n
                qproj_T = sT.n
            with ExitStack() as rAT:
                kbuf = [SR(f"kb{i}", [128, SEQ], BF16, rAT) for i in range(2)]
                vbuf = [SR(f"vb{i}", [128, 32 * 129], BF16, rAT) for i in range(2)]
                ptile = [SR(f"pt{i}", [128, T], BF16, rAT) for i in range(2)]
                ctxrow = [SR(f"cr{i}", [128, T], BF16, rAT) for i in range(2)]

                def v_ones(e):
                    for i in range(2):
                        sV.bump(e.memset(
                            vbuf[i][:, :].rearrange("p (g c) -> p g c", c=129)
                            [:, :, 128:129], 1.0))
                block.vector(v_ones)
                ones_v = sV.n

                NIT = NH * NTC
                sgrp = [0] * (4 * NIT)
                pvgrp = [0] * (4 * NIT)
                expgrp = [0] * (4 * NIT)
                ctx_ev = [0] * NIT
                hctr = [0] * NH
                cev_snap = [0] * NH

                for h in range(NH):
                    def y_kv(e, hh=h):
                        if hh == 0:
                            e.wait_ge(ccK.h, 1)
                            e.wait_ge(ccV.h, 1)
                        if hh < 2:
                            e.wait_ge(sT.h, qproj_T)
                            e.wait_ge(sV.h, q_ready)
                            e.wait_ge(sS.h, q_sS)
                        else:
                            e.wait_ge(kvf.h, hh - 1)
                        base = kvst.ni
                        for r in range(NRANK):
                            kvst.issue(
                                e, kbuf[hh % 2][:, T * r:T * (r + 1)],
                                kgo.rearrange("(r mc p) t -> r mc p t",
                                              mc=NKC, p=128)[r, hh])
                        for r in range(NRANK):
                            kvst.issue(
                                e,
                                vbuf[hh % 2][:, :].rearrange("p (g c) -> p g c", c=129)
                                [:, 8 * r:8 * (r + 1), 0:128],
                                vgo.rearrange("(r h j p) d -> r h p j d",
                                              h=NH, j=8, p=128)[r, hh])
                        for j in range(8):
                            kvst.wait(e, base + j)
                        e.sem_inc(kvdone.h, 1)
                    block.sync(y_kv)
                    kvdone.n += 1

                attn_join_s = sS.n
                for h in range(NH):
                    for qt in range(NTC):
                        it = h * NTC + qt
                        exp_base = sS.n
                        for g in range(4):
                            expgrp[4 * it + g] = exp_base + g + 1

                        def t_attn(e, hh=h, q=qt, i=it):
                            if i == 0:
                                e.wait_ge(sV.h, max(q_ready, ones_v))
                                e.wait_ge(sS.h, attn_join_s)
                            if q == 0:
                                e.wait_ge(kvdone.h, hh + 1)
                            for g in range(4):
                                gg = 4 * i + g
                                ps = pA if gg % 2 == 0 else pB
                                if gg >= 2:
                                    e.wait_ge(sS.h, expgrp[gg - 2])
                                for s_ in range(8):
                                    kt = 8 * g + s_
                                    sT.bump(e.matmul(
                                        ps[:, 128 * s_:128 * (s_ + 1)],
                                        kbuf[hh % 2][:, 128 * kt:128 * (kt + 1)],
                                        qall[:, T * hh + 128 * q:T * hh + 128 * (q + 1)],
                                        start=True, stop=True))
                                sgrp[gg] = sT.n
                            pctx = pF[:, 0:129] if i % 2 == 0 else pC[:, 0:129]
                            for g in range(4):
                                gg = 4 * i + g
                                e.wait_ge(sS.h, expgrp[gg])
                                if g == 0 and i >= 2:
                                    e.wait_ge(sV.h, ctx_ev[i - 2])
                                for s_ in range(8):
                                    kt = 8 * g + s_
                                    sT.bump(e.matmul(
                                        pctx,
                                        ptile[gg % 2][:, 128 * s_:128 * (s_ + 1)],
                                        vbuf[hh % 2][:, 129 * kt:129 * kt + 129],
                                        start=(g == 0 and s_ == 0),
                                        stop=(g == 3 and s_ == 7)))
                                pvgrp[gg] = sT.n
                        block.tensor(t_attn)

                        def s_exp(e, i=it):
                            for g in range(4):
                                gg = 4 * i + g
                                ps = pA if gg % 2 == 0 else pB
                                e.wait_ge(sT.h, sgrp[gg])
                                if gg >= 2:
                                    e.wait_ge(sT.h, pvgrp[gg - 2])
                                sS.bump(e.activation(out=ptile[gg % 2][:, :],
                                                     in_=ps[:, :], func=AF.Exp,
                                                     bias=consts[:, 1:2], scale=1.0))
                        block.scalar(s_exp)
                        assert sS.n == exp_base + 4

                        def v_ctx(e, hh=h, q=qt, i=it, w=sT.n,
                                  wtr=(hctr[h - 2] if (h >= 2 and qt == 0) else 0)):
                            e.wait_ge(sT.h, w)
                            if q == NTC - 1:
                                e.nop().then_inc(kvf.h, 1)
                            if wtr:
                                e.wait_ge(sT.h, wtr)
                            pctx = pF[:, 0:129] if i % 2 == 0 else pC[:, 0:129]
                            e.wait_ge(sV.h, sV.n)
                            sV.bump(e.reciprocal(recl[:, :], pctx[:, 128:129]))
                            e.wait_ge(sV.h, sV.n)
                            sV.bump(e.tensor_scalar(
                                out=ctxrow[hh % 2][:, 128 * q:128 * (q + 1)],
                                in0=pctx[:, 0:128], scalar1=recl[:, 0:1],
                                scalar2=None, op0=ALU.mult))
                        block.vector(v_ctx)
                        ctx_ev[it] = sV.n
                    def t_ctr(e, hh=h, w=sV.n,
                              wev=(cev_snap[h - 1] if h >= 1 else 0)):
                        e.wait_ge(sV.h, w)
                        if wev:
                            e.wait_ge(sV.h, wev)
                        for q2 in range(NTC):
                            sT.bump(e.transpose(
                                pE[:, 128 * q2:128 * (q2 + 1)],
                                ctxrow[hh % 2][:, 128 * q2:128 * (q2 + 1)],
                                ident[:, :]))
                    block.tensor(t_ctr)
                    hctr[h] = sT.n

                    def v_cev(e, hh=h, w=sT.n):
                        e.wait_ge(sT.h, w)
                        sV.bump(e.tensor_copy(out=cfm[:, T * hh:T * (hh + 1)],
                                              in_=pE[:, :]))
                    block.vector(v_cev)
                    cev_snap[h] = sV.n
                attn_T = sT.n

            with ExitStack() as rO:
                kres = SR("kreso", [128, NKC * D], BF16, rO)
                xinb = [SR(f"xia{i}", [128, D], F32, rO) for i in range(2)]
                xoutb = [SR(f"xoa{i}", [128, D], F32, rO) for i in range(2)]

                ctx_done = sV.n

                def g_wo(e):
                    e.wait_ge(sT.h, attn_T)
                    e.wait_ge(sV.h, ctx_done)
                    for k in range(NKC):
                        kst.issue(e, kres[:, D * k:D * (k + 1)],
                                  wo_d[:, D * k:D * (k + 1)])
                    kst.join(e)
                    e.sem_inc(kresdone.h, 1)
                block.gpsimd(g_wo)
                kresdone.n += 1
                kres_wo = kresdone.n

                def o_mm(e, t):
                    if t == 0:
                        e.wait_ge(kresdone.h, kres_wo)
                        e.wait_ge(sV.h, ctx_done)
                        e.wait_ge(sS.h, sS.n)
                    for n in range(4):
                        ps = pA if n < 2 else pB
                        psl = ps[:, 512 * (n % 2):512 * (n % 2 + 1)]
                        for kc in range(NKC):
                            mm = e.matmul(
                                psl,
                                cfm[:, T * kc + 128 * t:T * kc + 128 * (t + 1)],
                                kres[:, D * kc + 512 * n:D * kc + 512 * (n + 1)],
                                start=(kc == 0), stop=(kc == NKC - 1))
                            if kc == NKC - 1:
                                sT.bump(mm)

                def o_ev(e, t):
                    sV.bump(e.tensor_tensor(out=xoutb[t % 2][:, 0:1024], in0=pA[:, :],
                                            in1=xinb[t % 2][:, 0:1024], op=ALU.add))
                    sV.bump(e.tensor_tensor(out=xoutb[t % 2][:, 1024:2048],
                                            in0=pB[:, :],
                                            in1=xinb[t % 2][:, 1024:2048],
                                            op=ALU.add))
                resid_tm(xrows, xdrows, xinb, xoutb, o_mm, o_ev,
                         gate_t=attn_T, gate_v=ctx_done)

        def ffn_sublayer(tag, src_rows, dst_rows):
            with ExitStack() as ph:
                fmA = SR(f"fmA{tag}", [128, NKC * T], BF16, ph)
                entry_T = sT.n
                with ExitStack() as rB:
                    lnb = [SR(f"lnb{tag}{i}", [128, D], F32, rB) for i in range(2)]
                    lnbb = [SR(f"lnbb{tag}{i}", [128, D], BF16, rB) for i in range(2)]
                    ln_phase(src_rows, fmA, lnb, lnbb)
                fm_ready = sV.n

                hdn = SR(f"hdn{tag}", [128, NIC * T], BF16, ph)
                silu_t = SR(f"sil{tag}", [128, T], F32, ph)
                usc_t = SR(f"usc{tag}", [128, T], F32, ph)
                dnring = [SR(f"dnr{tag}{i}", [128, INTER], BF16, ph) for i in range(3)]
                dstile = [SR(f"dt{tag}{i}", [128, T], BF16, ph) for i in range(2)]
                stage = [SR(f"stg{tag}{i}", [128, T], F32, ph) for i in range(2)]

                gu_i = []

                def g_gu(e):
                    for ic in range(NIC):
                        gu_i.append(mst.issue(e, mring[mst.ni % 3][:, :],
                                              gate_d[:, D * ic:D * (ic + 1)]))
                        gu_i.append(mst.issue(e, mring[mst.ni % 3][:, :],
                                              up_d[:, D * ic:D * (ic + 1)]))
                block.gpsimd(g_gu)

                silu_snap = [0] * NIC
                usc_snap = [0] * NIC
                mult_snap = [0] * NIC
                join_s0 = sS.n
                for ic in range(NIC):
                    def t_g(e, i=ic, wi=gu_i[2 * ic]):
                        mst.wait(e, wi)
                        if i == 0:
                            e.wait_ge(sV.h, fm_ready)
                            e.wait_ge(sS.h, join_s0)
                        else:
                            e.wait_ge(sS.h, silu_snap[i - 1])
                        pan = mring[wi % 3]
                        for n in range(2):
                            for kc in range(NKC):
                                mm = e.matmul(
                                    pA[:, 512 * n:512 * (n + 1)],
                                    pan[:, 128 * kc:128 * (kc + 1)],
                                    fmA[:, T * kc + 512 * n:T * kc + 512 * (n + 1)],
                                    start=(kc == 0), stop=(kc == NKC - 1))
                                if kc == NKC - 1:
                                    sT.bump(mm)
                    block.tensor(t_g)
                    tg = sT.n

                    def t_u(e, i=ic, wi=gu_i[2 * ic + 1]):
                        mst.wait(e, wi)
                        if i >= 1:
                            e.wait_ge(sS.h, usc_snap[i - 1])
                        pan = mring[wi % 3]
                        for n in range(2):
                            for kc in range(NKC):
                                mm = e.matmul(
                                    pB[:, 512 * n:512 * (n + 1)],
                                    pan[:, 128 * kc:128 * (kc + 1)],
                                    fmA[:, T * kc + 512 * n:T * kc + 512 * (n + 1)],
                                    start=(kc == 0), stop=(kc == NKC - 1))
                                if kc == NKC - 1:
                                    sT.bump(mm)
                    block.tensor(t_u)
                    tu = sT.n

                    def s_silu(e, i=ic, wg=tg, wu=tu):
                        e.wait_ge(sT.h, wg)
                        e.nop().then_inc(mst.freesem, 1)
                        if i >= 1:
                            e.wait_ge(sV.h, mult_snap[i - 1])
                        sS.bump(e.activation(out=silu_t[:, :], in_=pA[:, :],
                                             func=AF.Silu, scale=gs_t[:, i:i + 1]))
                        silu_snap[i] = sS.n
                        e.wait_ge(sT.h, wu)
                        e.nop().then_inc(mst.freesem, 1)
                        sS.bump(e.activation(out=usc_t[:, :], in_=pB[:, :],
                                             func=AF.Copy, scale=us_t[:, i:i + 1]))
                        usc_snap[i] = sS.n
                    block.scalar(s_silu)

                    def v_hdn(e, i=ic):
                        e.wait_ge(sS.h, usc_snap[i])
                        sV.bump(e.tensor_tensor(out=hdn[:, T * i:T * (i + 1)],
                                                in0=silu_t[:, :], in1=usc_t[:, :],
                                                op=ALU.mult))
                    block.vector(v_hdn)
                    mult_snap[ic] = sV.n
                hdn_ready = sV.n

                dn_i = []
                postln_Tf = sT.n

                def g_dn(e, w=None):
                    e.wait_ge(sT.h, postln_Tf)
                    ost.join(e)
                    for mc in range(NKC):
                        dn_i.append(dstm.issue(e, dnring[dstm.ni % 3][:, :],
                                               down_d[:, INTER * mc:INTER * (mc + 1)]))
                block.gpsimd(g_dn)

                dsev_snap = [0] * NKC
                add_snap = [0] * NKC
                wb_idx = [None] * NKC
                join_s1 = sS.n
                for mc in range(NKC):
                    def t_d(e, m=mc, wi=dn_i[mc]):
                        dstm.wait(e, wi)
                        if m == 0:
                            e.wait_ge(sV.h, hdn_ready)
                            e.wait_ge(sS.h, join_s1)
                        elif m >= 2:
                            e.wait_ge(sS.h, dsev_snap[m - 2])
                        ps = pA if m % 2 == 0 else pB
                        pan = dnring[wi % 3]
                        for n in range(2):
                            for kc in range(NIC):
                                mm = e.matmul(
                                    ps[:, 512 * n:512 * (n + 1)],
                                    pan[:, 128 * kc:128 * (kc + 1)],
                                    hdn[:, T * kc + 512 * n:T * kc + 512 * (n + 1)],
                                    start=(kc == 0), stop=(kc == NIC - 1))
                                if kc == NIC - 1:
                                    sT.bump(mm)
                    block.tensor(t_d)
                    td = sT.n

                    def s_ds(e, m=mc, w=td):
                        e.wait_ge(sT.h, w)
                        e.nop().then_inc(dstm.freesem, 1)
                        if m >= 2:
                            e.wait_ge(sV.h, add_snap[m - 2])
                        ps = pA if m % 2 == 0 else pB
                        sS.bump(e.activation(out=dstile[m % 2][:, :], in_=ps[:, :],
                                             func=AF.Copy, scale=ds_t[:, m:m + 1]))
                    block.scalar(s_ds)
                    dsev_snap[mc] = sS.n

                    def y_rmwl(e, m=mc,
                               wwb=(wb_idx[mc - 2] if mc >= 2 else None)):
                        if wwb is not None:
                            ost.wait(e, wwb)
                        xst.issue(
                            e,
                            stage[m % 2][:, :].rearrange("p (j q) -> p j q", q=128),
                            src_rows.transpose([1, 0, 2])[:, :, 128 * m:128 * (m + 1)])
                    block.sync(y_rmwl)
                    li = xst.ni - 1

                    def t_tr(e, m=mc, w=dsev_snap[mc]):
                        e.wait_ge(sS.h, w)
                        if m >= 1:
                            e.wait_ge(sV.h, add_snap[m - 1])
                        for j in range(8):
                            sT.bump(e.transpose(
                                pE[:, 128 * j:128 * (j + 1)],
                                dstile[m % 2][:, 128 * j:128 * (j + 1)],
                                ident[:, :]))
                    block.tensor(t_tr)

                    def v_add(e, m=mc, w=sT.n, i=li,
                              wwb=(wb_idx[mc - 2] if mc >= 2 else None)):
                        e.wait_ge(sT.h, w)
                        xst.wait(e, i)
                        if wwb is not None:
                            ost.wait(e, wwb)
                        sV.bump(e.tensor_tensor(out=stage[m % 2][:, :], in0=pE[:, :],
                                                in1=stage[m % 2][:, :], op=ALU.add))
                        e.wait_ge(sV.h, sV.n)
                        e.sem_inc(xst.freesem, 1)
                    block.vector(v_add)
                    add_snap[mc] = sV.n

                    def y_wb(e, m=mc, w=sV.n):
                        e.wait_ge(sV.h, w)
                        wb_idx[m] = ost.issue(
                            e,
                            dst_rows.transpose([1, 0, 2])[:, :, 128 * m:128 * (m + 1)],
                            stage[m % 2][:, :].rearrange("p (j q) -> p j q", q=128))
                    block.sync(y_wb)

        ffn_sublayer(1, xdrows, xdrows)

        with ExitStack() as ph:
            fmA = SR("fmA2", [128, NKC * T], BF16, ph)
            entry_T = sT.n
            with ExitStack() as rB:
                lnb = [SR(f"lnb2{i}", [128, D], F32, rB) for i in range(2)]
                lnbb = [SR(f"lnbb2{i}", [128, D], BF16, rB) for i in range(2)]
                ln_phase(xdrows, fmA, lnb, lnbb)
            fm_ready = sV.n
            postln_T2 = sT.n

            kres = SR("kres2", [128, NKC * D], BF16, ph)

            def g_bk(e, w=None):
                e.wait_ge(sT.h, postln_T2)
                e.wait_ge(sV.h, fm_ready)
                for k in range(NKC):
                    kst.issue(e, kres[:, D * k:D * (k + 1)], bk_d[:, D * k:D * (k + 1)])
                kst.join(e)
                e.sem_inc(kresdone.h, 1)
            block.gpsimd(g_bk)
            kresdone.n += 1
            kres_bk = kresdone.n

            with ExitStack() as rKV:
                ktm = SR("ktm", [128, NTC * D], BF16, rKV)
                vtm = SR("vtm", [128, NTC * BH * 33], BF16, rKV)
                vkts = SR("vkts", [32, BH * 33], F32, rKV)

                def v_ones2(e, w=entry_T):
                    e.wait_ge(sT.h, w)
                    sV.bump(e.memset(
                        vtm[:, :].rearrange("p (t h c) -> p t h c", h=BH, c=33)
                        [:, :, :, 32:33], 1.0))
                block.vector(v_ones2)
                ones2_v = sV.n

                def tm_proj_b(batch_ready, dest_k, evsnap, join_s):
                    for tc in range(NTC):
                        def t_mm(e, t=tc):
                            e.wait_ge(kresdone.h, batch_ready)
                            if t == 0:
                                e.wait_ge(sV.h, max(fm_ready, ones2_v))
                                e.wait_ge(sS.h, join_s)
                            else:
                                e.wait_ge(sS.h, evsnap[t - 1])
                            for n in range(4):
                                ps = pA if n < 2 else pB
                                psl = ps[:, 512 * (n % 2):512 * (n % 2 + 1)]
                                for kc in range(NKC):
                                    mm = e.matmul(
                                        psl,
                                        fmA[:, T * kc + 128 * t:T * kc + 128 * (t + 1)],
                                        kres[:, D * kc + 512 * n:D * kc + 512 * (n + 1)],
                                        start=(kc == 0), stop=(kc == NKC - 1))
                                    if kc == NKC - 1:
                                        sT.bump(mm)
                        block.tensor(t_mm)

                        def s_ev(e, t=tc, w=sT.n):
                            e.wait_ge(sT.h, w)
                            if dest_k:
                                sS.bump(e.activation(out=ktm[:, D * t:D * t + 1024],
                                                     in_=pA[:, :], func=AF.Relu,
                                                     scale=1.0))
                                sS.bump(e.activation(
                                    out=ktm[:, D * t + 1024:D * (t + 1)],
                                    in_=pB[:, :], func=AF.Relu, scale=1.0))
                            else:
                                vv = vtm[:, 2112 * t:2112 * (t + 1)].rearrange(
                                    "p (h c) -> p h c", c=33)
                                sS.bump(e.activation(out=vv[:, 0:32, 0:32],
                                                     in_=pA[:, :].rearrange(
                                                         "p (h d) -> p h d", d=32),
                                                     func=AF.Copy, scale=1.0))
                                sS.bump(e.activation(
                                    out=vv[:, 32:64, 0:32],
                                    in_=pB[:, :].rearrange("p (h d) -> p h d", d=32),
                                    func=AF.Copy, scale=1.0))
                            evsnap[t] = sS.n
                        block.scalar(s_ev)

                kev = [0] * NTC
                tm_proj_b(kres_bk, True, kev, join_s=sS.n)
                kproj_T = sT.n

                def g_bv(e):
                    e.wait_ge(sT.h, kproj_T)
                    for k in range(NKC):
                        kst.issue(e, kres[:, D * k:D * (k + 1)],
                                  bv_d[:, D * k:D * (k + 1)])
                    kst.join(e)
                    e.sem_inc(kresdone.h, 1)
                block.gpsimd(g_bv)
                kresdone.n += 1

                vev = [0] * NTC
                tm_proj_b(kresdone.n, False, vev, join_s=sS.n)
                vproj_T = sT.n
                kv_ready = sS.n

                def g_bp(e):
                    e.wait_ge(sT.h, vproj_T)
                    for k in range(NKC):
                        kst.issue(e, kres[:, D * k:D * (k + 1)],
                                  bp_d[:, D * k:D * (k + 1)])
                    kst.join(e)
                    e.sem_inc(kresdone.h, 1)
                block.gpsimd(g_bp)
                kresdone.n += 1
                kres_bp = kresdone.n

                vkev = [0] * BH
                for h in range(BH):
                    def t_vk(e, hh=h):
                        if hh == 0:
                            e.wait_ge(sS.h, kv_ready)
                        elif hh >= 2:
                            e.wait_ge(sV.h, vkev[hh - 2])
                        pvk = pF[0:32, 0:33] if hh % 2 == 0 else pC[0:32, 0:33]
                        for t in range(NTC):
                            mm = e.matmul(
                                pvk,
                                ktm[:, D * t + 32 * hh:D * t + 32 * (hh + 1)],
                                vtm[:, 2112 * t + 33 * hh:2112 * t + 33 * (hh + 1)],
                                start=(t == 0), stop=(t == NTC - 1))
                            if t == NTC - 1:
                                sT.bump(mm)
                    block.tensor(t_vk)

                    def v_vkev(e, hh=h, w=sT.n):
                        e.wait_ge(sT.h, w)
                        pvk = pF[0:32, 0:33] if hh % 2 == 0 else pC[0:32, 0:33]
                        sV.bump(e.tensor_copy(out=vkts[:, 33 * hh:33 * (hh + 1)],
                                              in_=pvk))
                    block.vector(v_vkev)
                    vkev[h] = sV.n

                def y_vkout(e, w=sV.n):
                    e.wait_ge(sV.h, w)
                    i = yst.issue(e, vki, vkts[:, :])
                    yst.wait(e, i)
                    e.sem_inc(ydone.h, 1)
                block.sync(y_vkout)
                ydone.n += 1

                def g_ccR(e, w=ydone.n):
                    e.wait_ge(ydone.h, w)
                    e.collective_compute("AllReduce", ALU.add, replica_groups=GROUPS,
                                         ins=[vki], outs=[vko]).then_inc(ccR.h, 1)
                block.gpsimd(g_ccR)
                ccR.n += 1
                vk_T = sT.n

            with ExitStack() as rQ:
                fmB = SR("fmB", [128, NKC * T], BF16, rQ)
                vkt4 = SR("vkt4", [128, BH * 33], BF16, rQ)
                obsb = [SR(f"ob{i}", [32, T], F32, rQ) for i in range(2)]
                dden = SR("dden", [33, T], F32, rQ)
                drecb = SR("drecb", [33, T], BF16, rQ)

                def v_qzero(e):
                    e.wait_ge(ccR.h, 1)
                    e.wait_ge(sT.h, vk_T)
                    sV.bump(e.memset(vkt4[:, :], 0.0))
                    sV.bump(e.memset(drecb[:, :], 0.0))
                block.vector(v_qzero)
                qzero_v = sV.n

                def g_vkt4(e):
                    e.wait_ge(ccR.h, 1)
                    e.wait_ge(sT.h, vk_T)
                    e.wait_ge(sV.h, qzero_v)
                    base = kst.ni
                    for j in range(4):
                        kst.issue(e, vkt4[32 * j:32 * (j + 1), :].rearrange(
                            "p (b r c) -> p b r c", r=4, c=33)[:, :, j, :],
                            vko.rearrange("p (b j c) -> p b j c", j=4, c=33)[:, :, j, :])
                    for j in range(4):
                        kst.wait(e, base + j)
                    e.sem_inc(ydone.h, 1)
                block.gpsimd(g_vkt4)
                ydone.n += 1
                vkt4_ready = ydone.n

                bq_i = []

                def g_bq(e):
                    for m in range(NKC):
                        bq_i.append(mst.issue(e, mring[mst.ni % 3][:, :],
                                              bq_d[:, D * m:D * (m + 1)]))
                block.gpsimd(g_bq)

                qrelu = [0] * NKC
                join_q = sS.n
                for mc in range(NKC):
                    def t_q(e, m=mc, wi=bq_i[mc]):
                        mst.wait(e, wi)
                        if m == 0:
                            e.wait_ge(sV.h, fm_ready)
                            e.wait_ge(sS.h, join_q)
                        elif m >= 2:
                            e.wait_ge(sS.h, qrelu[m - 2])
                        ps = pA if m % 2 == 0 else pB
                        pan = mring[wi % 3]
                        for n in range(2):
                            for kc in range(NKC):
                                mm = e.matmul(
                                    ps[:, 512 * n:512 * (n + 1)],
                                    pan[:, 128 * kc:128 * (kc + 1)],
                                    fmA[:, T * kc + 512 * n:T * kc + 512 * (n + 1)],
                                    start=(kc == 0), stop=(kc == NKC - 1))
                                if kc == NKC - 1:
                                    sT.bump(mm)
                    block.tensor(t_q)

                    def s_q(e, m=mc, w=sT.n, wg=vk_T):
                        e.wait_ge(sT.h, max(w, wg))
                        e.nop().then_inc(mst.freesem, 1)
                        ps = pA if m % 2 == 0 else pB
                        sS.bump(e.activation(out=fmB[:, T * m:T * (m + 1)],
                                             in_=ps[:, :], func=AF.Relu, scale=1.0))
                    block.scalar(s_q)
                    qrelu[mc] = sS.n
                q_ready2 = sS.n

                outbuf = fmA
                ob_ev = [0] * BH
                ob_mult = [0] * BH
                den_snap = [0] * BH
                for h in range(BH):
                    def t_ob(e, hh=h):
                        if hh == 0:
                            e.wait_ge(ydone.h, vkt4_ready)
                            e.wait_ge(sS.h, q_ready2)
                        elif hh >= 2:
                            e.wait_ge(sS.h, ob_ev[hh - 2])
                            e.wait_ge(sV.h, den_snap[hh - 2])
                        ps = pA if hh % 2 == 0 else pB
                        r, b = hh % 4, hh // 4
                        for n in range(2):
                            sT.bump(e.matmul(
                                ps[0:33, 512 * n:512 * (n + 1)],
                                vkt4[:, 33 * hh:33 * (hh + 1)],
                                fmB[:, T * b + 512 * n:T * b + 512 * (n + 1)],
                                start=True, stop=True))
                    block.tensor(t_ob)
                    tob = sT.n

                    def v_den(e, hh=h, w=tob):
                        e.wait_ge(sT.h, w)
                        ps = pA if hh % 2 == 0 else pB
                        sV.bump(e.tensor_scalar(out=dden[32:33, :],
                                                in0=ps[32:33, 0:1024],
                                                scalar1=1e-15, scalar2=1e-12,
                                                op0=ALU.add, op1=ALU.max))
                        e.wait_ge(sV.h, sV.n)
                        sV.bump(e.reciprocal(dden[32:33, :], dden[32:33, :]))
                        e.wait_ge(sV.h, sV.n)
                        sV.bump(e.tensor_copy(out=drecb[32:33, :], in_=dden[32:33, :]))
                    block.vector(v_den)
                    den_snap[h] = sV.n

                    def t_bc(e, hh=h, w=sV.n):
                        e.wait_ge(sV.h, w)
                        if hh >= 1:
                            e.wait_ge(sV.h, ob_mult[hh - 1])
                        for n in range(2):
                            sT.bump(e.matmul(pC[0:32, 512 * n:512 * (n + 1)],
                                             ones33[:, :],
                                             drecb[0:33, 512 * n:512 * (n + 1)],
                                             start=True, stop=True))
                    block.tensor(t_bc)
                    tbc = sT.n

                    def s_ob(e, hh=h, w=tob):
                        e.wait_ge(sT.h, w)
                        if hh >= 2:
                            e.wait_ge(sV.h, ob_mult[hh - 2])
                        ps = pA if hh % 2 == 0 else pB
                        sS.bump(e.activation(out=obsb[hh % 2][:, :],
                                             in_=ps[0:32, 0:1024],
                                             func=AF.Copy, scale=1.0))
                    block.scalar(s_ob)
                    ob_ev[h] = sS.n

                    def v_obm(e, hh=h, w=sS.n, wb=tbc):
                        e.wait_ge(sS.h, w)
                        e.wait_ge(sT.h, wb)
                        r, b = hh % 4, hh // 4
                        sV.bump(e.tensor_tensor(
                            out=outbuf[32 * r:32 * (r + 1), T * b:T * (b + 1)],
                            in0=obsb[hh % 2][:, :], in1=pC[0:32, :], op=ALU.mult))
                    block.vector(v_obm)
                    ob_mult[h] = sV.n
                outb_ready = sV.n
                oe_T = sT.n

            with ExitStack() as rP:
                bpb_t = SR("bpb_t", [128, D], F32, rP)
                xinb = [SR(f"xib{i}", [128, D], F32, rP) for i in range(2)]
                xoutb = [SR(f"xob{i}", [128, D], F32, rP) for i in range(2)]

                def y_bpb(e):
                    e.wait_ge(sT.h, oe_T)
                    e.wait_ge(sV.h, outb_ready)
                    i = yst.issue(e, bpb_t[:, :], bpb_d)
                    yst.wait(e, i)
                    e.sem_inc(ydone.h, 1)
                block.sync(y_bpb)
                ydone.n += 1
                bpb_ready = ydone.n

                def p_mm(e, t):
                    if t == 0:
                        e.wait_ge(kresdone.h, kres_bp)
                        e.wait_ge(sV.h, outb_ready)
                        e.wait_ge(sS.h, sS.n)
                    for n in range(4):
                        ps = pA if n < 2 else pB
                        psl = ps[:, 512 * (n % 2):512 * (n % 2 + 1)]
                        for kc in range(NKC):
                            mm = e.matmul(
                                psl,
                                outbuf[:, T * kc + 128 * t:T * kc + 128 * (t + 1)],
                                kres[:, D * kc + 512 * n:D * kc + 512 * (n + 1)],
                                start=(kc == 0), stop=(kc == NKC - 1))
                            if kc == NKC - 1:
                                sT.bump(mm)

                def p_ev(e, t):
                    e.wait_ge(ydone.h, bpb_ready)
                    for half, ps in ((0, pA), (1, pB)):
                        sl = slice(1024 * half, 1024 * (half + 1))
                        sV.bump(e.tensor_tensor(out=tt1[:, :], in0=ps[:, :],
                                                in1=bpb_t[:, sl], op=ALU.add))
                        e.wait_ge(sV.h, sV.n)
                        sV.bump(e.tensor_tensor(out=xoutb[t % 2][:, sl], in0=tt1[:, :],
                                                in1=xinb[t % 2][:, sl], op=ALU.add))
                        e.wait_ge(sV.h, sV.n)
                resid_tm(xdrows, xdrows, xinb, xoutb, p_mm, p_ev,
                         gate_t=oe_T, gate_v=outb_ready)

        ffn_sublayer(3, xdrows, orows)

        def y_final(e):
            ost.join(e)
        block.sync(y_final)

    return nc


def _prep_weights(inputs):
    wq = _mpan(np.ascontiguousarray(inputs["L0_attn_a_Wq"]))
    wk = _mpan(np.ascontiguousarray(inputs["L0_attn_a_Wk"]))
    wv = _kpan(np.ascontiguousarray(inputs["L0_attn_a_Wv"]))
    wo = _kpan(np.ascontiguousarray(inputs["L0_attn_a_Wo"]))
    gate = _mpan(np.ascontiguousarray(np.asarray(inputs["L0_ffn_gate_w"]).T))
    up = _mpan(np.ascontiguousarray(np.asarray(inputs["L0_ffn_up_w"]).T))
    down = _mpan(np.ascontiguousarray(np.asarray(inputs["L0_ffn_down_w"]).T))
    wqkv = np.asarray(inputs["L0_attn_b_w_qkv"])
    bq = _mpan(np.ascontiguousarray(wqkv[0:D].T))
    bk = _kpan(np.ascontiguousarray(wqkv[D:2 * D].T))
    bv = _kpan(np.ascontiguousarray(wqkv[2 * D:3 * D].T))
    bp = _kpan(np.ascontiguousarray(np.asarray(inputs["L0_attn_b_w_proj"]).T))
    gs = np.ascontiguousarray(np.asarray(inputs["L0_ffn_gate_s"]).reshape(NIC, 128).T)
    us = np.ascontiguousarray(np.asarray(inputs["L0_ffn_up_s"]).reshape(NIC, 128).T)
    ds = np.ascontiguousarray(np.asarray(inputs["L0_ffn_down_s"]).reshape(NKC, 128).T)
    bpb = np.ascontiguousarray(
        np.broadcast_to(np.asarray(inputs["L0_attn_b_b_proj"])[None, :], (128, D)))
    return dict(wq=wq, wk=wk, wv=wv, wo=wo, gate=gate, up=up, down=down,
                bq=bq, bk=bk, bv=bv, bp=bp, gs=gs, us=us, ds=ds, bpb=bpb)


def kernel(**inputs):
    shared = _prep_weights(inputs)
    shared = {k: np.asarray(v, dtype=np.float32) for k, v in shared.items()}
    xfull = np.asarray(inputs["L0_attn_a_hidden_states"], dtype=np.float32)
    nc = build_nc()
    in_maps = []
    for c in range(8):
        b, s0 = c // 4, (c % 4) * T
        m = dict(shared)
        m["x"] = np.ascontiguousarray(xfull[b, s0:s0 + T])
        m["poff"] = np.full((128, 1), float(s0), np.float32)
        in_maps.append(m)
    res = run_bass_kernel_spmd(nc, in_maps, core_ids=list(range(8)))
    out = np.empty((2, SEQ, D), np.float32)
    for c in range(8):
        b, s0 = c // 4, (c % 4) * T
        out[b, s0:s0 + T] = res.results[c]["out"]
    return out


# version 07
# speedup vs baseline: 1.0053x; 1.0053x over previous
import numpy as np
import concourse.bass as bass
import concourse.mybir as mybir
from concourse.bass_utils import run_bass_kernel_spmd
from contextlib import ExitStack

F32 = mybir.dt.float32
BF16 = mybir.dt.bfloat16
I32 = mybir.dt.int32
AF = mybir.ActivationFunctionType
ALU = mybir.AluOpType
PI = float(np.pi)

D, T, NTC, NKC, NH = 2048, 1024, 8, 16, 16
INTER, NIC, BH = 5632, 44, 64
SEQ, NRANK = 4096, 4
GROUPS = [[0, 1, 2, 3], [4, 5, 6, 7]]
ISQ = 1.0 / float(np.sqrt(128.0))
EXPSHIFT = -20.0


class Sem:
    def __init__(self, st, nc, name):
        self.h = st.enter_context(nc.semaphore(name))
        self.n = 0

    def bump(self, instr, k=1):
        instr.then_inc(self.h, k)
        self.n += k


class Stream:
    def __init__(self, st, nc, name, nsem=4, depth=None, gate=None):
        self.sems = [st.enter_context(nc.semaphore(f"{name}c{j}")) for j in range(nsem)]
        self.freesem = st.enter_context(nc.semaphore(f"{name}f"))
        self.nsem, self.depth, self.gate, self.ni = nsem, depth, gate, 0

    def issue(self, e, out, in_):
        i = self.ni
        self.ni += 1
        if self.depth is not None and i >= self.depth:
            if self.gate == "free":
                e.wait_ge(self.freesem, i - self.depth + 1)
            elif self.gate == "self":
                self.wait(e, i - self.depth)
        e.dma_start(out=out, in_=in_).then_inc(self.sems[i % self.nsem], 16)
        return i

    def wait(self, e, i):
        e.wait_ge(self.sems[i % self.nsem], 16 * (i // self.nsem + 1))

    def join(self, e):
        for j in range(min(self.nsem, self.ni)):
            i = max(k for k in range(self.ni) if k % self.nsem == j)
            self.wait(e, i)


def _mpan(W):
    Kd, Md = W.shape
    kc, mc = Kd // 128, Md // 128
    return np.ascontiguousarray(
        W.reshape(kc, 128, mc, 128).transpose(1, 2, 0, 3).reshape(128, mc * kc * 128))


def _kpan(W):
    Kd, Md = W.shape
    kc = Kd // 128
    return np.ascontiguousarray(
        W.reshape(kc, 128, Md).transpose(1, 0, 2).reshape(128, kc * Md))


def build_nc():
    nc = bass.Bass("TRN2", target_bir_lowering=False)

    xe = nc.dram_tensor("x", [T, D], F32, kind="ExternalInput").ap()
    oe = nc.dram_tensor("out", [T, D], F32, kind="ExternalOutput").ap()
    wq_d = nc.dram_tensor("wq", [128, NKC * D], F32, kind="ExternalInput").ap()
    wk_d = nc.dram_tensor("wk", [128, NKC * D], F32, kind="ExternalInput").ap()
    wv_d = nc.dram_tensor("wv", [128, NKC * D], F32, kind="ExternalInput").ap()
    wo_d = nc.dram_tensor("wo", [128, NKC * D], F32, kind="ExternalInput").ap()
    gate_d = nc.dram_tensor("gate", [128, NIC * D], F32, kind="ExternalInput").ap()
    up_d = nc.dram_tensor("up", [128, NIC * D], F32, kind="ExternalInput").ap()
    down_d = nc.dram_tensor("down", [128, NKC * INTER], F32, kind="ExternalInput").ap()
    bq_d = nc.dram_tensor("bq", [128, NKC * D], F32, kind="ExternalInput").ap()
    bk_d = nc.dram_tensor("bk", [128, NKC * D], F32, kind="ExternalInput").ap()
    bv_d = nc.dram_tensor("bv", [128, NKC * D], F32, kind="ExternalInput").ap()
    bp_d = nc.dram_tensor("bp", [128, NKC * D], F32, kind="ExternalInput").ap()
    gs_d = nc.dram_tensor("gs", [128, NIC], F32, kind="ExternalInput").ap()
    us_d = nc.dram_tensor("us", [128, NIC], F32, kind="ExternalInput").ap()
    ds_d = nc.dram_tensor("ds", [128, NKC], F32, kind="ExternalInput").ap()
    bpb_d = nc.dram_tensor("bpb", [128, D], F32, kind="ExternalInput").ap()
    poff_d = nc.dram_tensor("poff", [128, 1], F32, kind="ExternalInput").ap()

    xd = nc.dram_tensor("xd", [T, D], F32).ap()
    kgi = nc.dram_tensor("kgi", [D, T], BF16).ap()
    kgo = nc.dram_tensor("kgo", [NRANK * D, T], BF16).ap()
    vgi = nc.dram_tensor("vgi", [NH * T, 128], BF16).ap()
    vgo = nc.dram_tensor("vgo", [NRANK * NH * T, 128], BF16).ap()
    vki = nc.dram_tensor("vki", [32, BH * 33], F32).ap()
    vko = nc.dram_tensor("vko", [32, BH * 33], F32).ap()

    xrows = xe.rearrange("(a p) d -> a p d", p=128)
    orows = oe.rearrange("(a p) d -> a p d", p=128)
    xdrows = xd.rearrange("(a p) d -> a p d", p=128)

    st = ExitStack()
    with st:
        def SL(name, shape, dt=F32, stk=st):
            return stk.enter_context(nc.sbuf_tensor(name, shape, dt, side="left"))

        def SR(name, shape, dt=F32, stk=st):
            return stk.enter_context(nc.sbuf_tensor(name, shape, dt, side="right"))

        mring = [SL(f"mr{i}", [128, D], BF16) for i in range(4)]
        tt1 = SL("tt1", [128, T])
        tt2 = SL("tt2", [128, T])
        consts = SL("consts", [128, 2])
        ident = SL("ident", [128, 128], BF16)
        shiftm = SL("shiftm", [128, 128], BF16)
        gs_t = SL("gs_t", [128, NIC])
        us_t = SL("us_t", [128, NIC])
        ds_t = SL("ds_t", [128, NKC])
        poff_t = SL("poff_t", [128, 1])
        bnacc = SL("bnacc", [128, 24])
        bnmv = SL("bnmv", [128, 2])
        rstd = SL("rstd", [128, 1])
        recl = SL("recl", [128, 1])
        ones33 = SL("ones33", [33, 32], BF16)

        pA = st.enter_context(nc.psum_tensor("pA", [128, 1024], F32))
        pB = st.enter_context(nc.psum_tensor("pB", [128, 1024], F32))
        pC = st.enter_context(nc.psum_tensor("pC", [128, 1024], F32))
        pF = st.enter_context(nc.psum_tensor("pF", [128, 512], F32))
        pE = st.enter_context(nc.psum_tensor("pE", [128, 1024], BF16))

        sT = Sem(st, nc, "sT")
        sV = Sem(st, nc, "sV")
        sS = Sem(st, nc, "sS")
        sG = Sem(st, nc, "sG")
        ccK = Sem(st, nc, "ccK")
        ccV = Sem(st, nc, "ccV")
        ccR = Sem(st, nc, "ccR")
        kvdone = Sem(st, nc, "kvdone")
        kvf = Sem(st, nc, "kvf")
        kresdone = Sem(st, nc, "kresd")
        ydone = Sem(st, nc, "ydone")

        mst = Stream(st, nc, "mst", nsem=4, depth=4, gate="free")
        kst = Stream(st, nc, "kst", nsem=4)
        dstm = Stream(st, nc, "dstm", nsem=4, depth=3, gate="free")
        yst = Stream(st, nc, "yst", nsem=8)
        cst = Stream(st, nc, "cst", nsem=4, depth=2, gate="self")
        xst = Stream(st, nc, "xst", nsem=4, depth=2, gate="free")
        ost = Stream(st, nc, "ost", nsem=4)
        kvst = Stream(st, nc, "kvst", nsem=8)

        block = st.enter_context(nc.Block())

        def g_setup(e):
            def bar(instr, wait=True):
                sG.bump(instr)
                if wait:
                    e.wait_ge(sG.h, sG.n)
            bar(e.memset(ident[:, :], 0.0))
            bar(e.affine_select(out=ident[:, :], in_=ident[:, :],
                                compare_op=ALU.not_equal, fill=1.0, base=0,
                                pattern=[[-1, 128]], channel_multiplier=1), False)
            bar(e.memset(shiftm[:, :], 0.0))
            bar(e.affine_select(out=shiftm[:, :], in_=shiftm[:, :],
                                compare_op=ALU.not_equal, fill=1.0, base=1,
                                pattern=[[-1, 128]], channel_multiplier=1))
            bar(e.affine_select(out=shiftm[:, :], in_=shiftm[:, :],
                                compare_op=ALU.not_equal, fill=1.0, base=-127,
                                pattern=[[-1, 128]], channel_multiplier=1), False)
            bar(e.memset(consts[:, 0:1], 1e-5), False)
            bar(e.memset(consts[:, 1:2], EXPSHIFT), False)
            bar(e.memset(ones33[:, :], 0.0))
            bar(e.memset(ones33[32:33, :], 1.0), False)
            bar(e.iota(tt1[:, :], [[1, T]], channel_multiplier=0,
                       allow_small_or_imprecise_dtypes=True), False)
            bar(e.iota(tt2[:, 0:1], [[1, 1]], channel_multiplier=1,
                       allow_small_or_imprecise_dtypes=True), False)
        block.gpsimd(g_setup)

        def y_small(e):
            for dv, sv_ in ((gs_t, gs_d), (us_t, us_d), (ds_t, ds_d), (poff_t, poff_d)):
                yst.issue(e, dv[:, :], sv_)
            yst.join(e)
            e.sem_inc(ydone.h, 1)
        block.sync(y_small)
        ydone.n += 1

        def v_invf(e):
            e.wait_ge(sG.h, sG.n)
            e.wait_ge(ydone.h, ydone.n)
            sV.bump(e.tensor_scalar(out=tt2[:, 1:2], in0=tt2[:, 0:1], scalar1=63.5,
                                    scalar2=None, op0=ALU.is_gt))
            e.wait_ge(sV.h, sV.n)
            sV.bump(e.tensor_scalar(out=tt2[:, 1:2], in0=tt2[:, 1:2], scalar1=64.0,
                                    scalar2=None, op0=ALU.mult))
            e.wait_ge(sV.h, sV.n)
            sV.bump(e.tensor_tensor(out=tt2[:, 0:1], in0=tt2[:, 0:1], in1=tt2[:, 1:2],
                                    op=ALU.subtract))
        block.vector(v_invf)

        def s_invf(e):
            e.wait_ge(sV.h, sV.n)
            sS.bump(e.activation(out=tt2[:, 0:1], in_=tt2[:, 0:1], func=AF.Exp,
                                 scale=-float(np.log(10000.0) / 64.0)))
        block.scalar(s_invf)

        def ln_phase(src_rows, fmA, lnb, lnbb):
            trsnap = [0, 0]
            for tc in range(NTC):
                def y_ld(e, t=tc):
                    if t == 0 and ost.ni:
                        ost.join(e)
                    xst.issue(e, lnb[t % 2][:, :], src_rows[t])
                block.sync(y_ld)
                li = xst.ni - 1

                def v_stats(e, t=tc, i=li):
                    xst.wait(e, i)
                    for j in range(4):
                        sV.bump(e.bn_stats(bnacc[:, 6 * j:6 * j + 6],
                                           lnb[t % 2][:, 512 * j:512 * (j + 1)]))
                block.vector(v_stats)

                def v_aggr(e):
                    e.wait_ge(sV.h, sV.n)
                    sV.bump(e.bn_aggr(bnmv[:, :], bnacc[:, :]))
                block.vector(v_aggr)

                def s_rstd(e):
                    e.wait_ge(sV.h, sV.n)
                    sS.bump(e.activation(out=rstd[:, :], in_=bnmv[:, 1:2], func=AF.Ln,
                                         bias=consts[:, 0:1], scale=1.0))
                    e.wait_ge(sS.h, sS.n)
                    sS.bump(e.activation(out=rstd[:, :], in_=rstd[:, :],
                                         func=AF.Exp, scale=-0.5))
                block.scalar(s_rstd)

                def v_norm(e, t=tc, wtr=trsnap[tc % 2]):
                    e.wait_ge(sS.h, sS.n)
                    if wtr:
                        e.wait_ge(sT.h, wtr)
                    sV.bump(e.tensor_scalar(
                        out=lnbb[t % 2][:, :], in0=lnb[t % 2][:, :],
                        scalar1=bnmv[:, 0:1], scalar2=rstd[:, 0:1],
                        op0=ALU.subtract, op1=ALU.mult))
                    e.wait_ge(sV.h, sV.n)
                    e.sem_inc(xst.freesem, 1)
                block.vector(v_norm)
                vnorm = sV.n

                for half in range(2):
                    def t_tr(e, t=tc, hf=half, w=max(vnorm, sV.n)):
                        e.wait_ge(sV.h, w)
                        for j in range(8):
                            dc = hf * 8 + j
                            sT.bump(e.transpose(pE[:, 128 * j:128 * (j + 1)],
                                                lnbb[t % 2][:, 128 * dc:128 * (dc + 1)],
                                                ident[:, :]))
                    block.tensor(t_tr)

                    def v_ev(e, t=tc, hf=half, w=sT.n):
                        e.wait_ge(sT.h, w)
                        dv = fmA[:, :].rearrange("p (dc q) -> p dc q", q=T)[
                            :, hf * 8:hf * 8 + 8, t * 128:t * 128 + 128]
                        sV.bump(e.tensor_copy(out=dv, in_=pE[:, :]))
                    block.vector(v_ev)
                trsnap[tc % 2] = sT.n

        def resid_tm(src_rows, dstrows, xinb, xoutb, emit_mm, emit_evac,
                     gate_t=0, gate_v=0):
            oidx = []
            for tc in range(NTC):
                def y_ld(e, t=tc):
                    if t == 0:
                        if gate_t:
                            e.wait_ge(sT.h, gate_t)
                        if gate_v:
                            e.wait_ge(sV.h, gate_v)
                    xst.issue(e, xinb[t % 2][:, :], src_rows[t])
                block.sync(y_ld)
                li = xst.ni - 1

                def t_mm(e, t=tc, w=sV.n):
                    if t >= 1:
                        e.wait_ge(sV.h, w)
                    emit_mm(e, t)
                block.tensor(t_mm)

                def v_ev(e, t=tc, i=li, w=sT.n,
                         wo_=(oidx[tc - 2] if tc >= 2 else None)):
                    e.wait_ge(sT.h, w)
                    xst.wait(e, i)
                    if wo_ is not None:
                        ost.wait(e, wo_)
                    emit_evac(e, t)
                    e.wait_ge(sV.h, sV.n)
                    e.sem_inc(xst.freesem, 1)
                block.vector(v_ev)

                def y_stv(e, t=tc, w=sV.n):
                    e.wait_ge(sV.h, w)
                    oidx.append(ost.issue(e, dstrows[t], xoutb[t % 2][:, :]))
                block.sync(y_stv)

        with ExitStack() as ph:
            qall = SL("qall", [128, NH * T], BF16, ph)
            cfm = SL("cfm", [128, NKC * T], BF16, ph)
            with ExitStack() as rA:
                fmA = SR("fmA0", [128, NKC * T], BF16, rA)
                tabs = SR("tabs", [128, 4 * T], F32, rA)
                krow = [SR(f"krw{i}", [128, T], BF16, rA) for i in range(2)]

                with ExitStack() as rB:
                    ti32 = SR("ti32", [128, T], I32, rB)
                    lnb = [SR(f"lnb0{i}", [128, D], F32, rB) for i in range(2)]
                    lnbb = [SR(f"lnbb0{i}", [128, D], BF16, rB) for i in range(2)]

                    def v_tab(e):
                        e.wait_ge(sS.h, sS.n)

                        def bar(instr):
                            sV.bump(instr)
                            e.wait_ge(sV.h, sV.n)
                        bar(e.tensor_scalar(out=tt1[:, :], in0=tt1[:, :],
                                            scalar1=poff_t[:, 0:1], scalar2=tt2[:, 0:1],
                                            op0=ALU.add, op1=ALU.mult))
                        for half, shift in ((0, PI / 2.0), (1, 0.0)):
                            dv = tabs[:, half * 2 * T:half * 2 * T + T]
                            bar(e.tensor_scalar(out=dv, in0=tt1[:, :], scalar1=shift,
                                                scalar2=1.0 / (2 * PI),
                                                op0=ALU.add, op1=ALU.mult))
                            bar(e.tensor_copy(out=ti32[:, :], in_=dv))
                            bar(e.tensor_copy(out=tt2[:, :], in_=ti32[:, :]))
                            bar(e.tensor_tensor(out=dv, in0=dv, in1=tt2[:, :],
                                                op=ALU.subtract))
                            bar(e.tensor_scalar(out=tt2[:, :], in0=dv, scalar1=0.5,
                                                scalar2=None, op0=ALU.is_gt))
                            bar(e.tensor_tensor(out=dv, in0=dv, in1=tt2[:, :],
                                                op=ALU.subtract))
                            bar(e.tensor_scalar(out=tt2[:, :], in0=dv, scalar1=-0.5,
                                                scalar2=None, op0=ALU.is_lt))
                            bar(e.tensor_tensor(out=dv, in0=dv, in1=tt2[:, :], op=ALU.add))
                            bar(e.tensor_scalar(out=dv, in0=dv, scalar1=2 * PI,
                                                scalar2=None, op0=ALU.mult))
                    block.vector(v_tab)

                    def s_tab(e):
                        e.wait_ge(sV.h, sV.n)
                        sS.bump(e.activation(out=tabs[:, 0:T], in_=tabs[:, 0:T],
                                             func=AF.Sin))
                        sS.bump(e.activation(out=tabs[:, 2 * T:3 * T],
                                             in_=tabs[:, 2 * T:3 * T], func=AF.Sin))
                    block.scalar(s_tab)

                    def v_tabq(e):
                        e.wait_ge(sS.h, sS.n)
                        e.tensor_scalar(out=tabs[:, T:2 * T], in0=tabs[:, 0:T],
                                        scalar1=ISQ, scalar2=None, op0=ALU.mult)
                        sV.bump(e.tensor_scalar(out=tabs[:, 3 * T:4 * T],
                                                in0=tabs[:, 2 * T:3 * T],
                                                scalar1=ISQ, scalar2=None,
                                                op0=ALU.mult))
                    block.vector(v_tabq)

                    ln_phase(xrows, fmA, lnb, lnbb)
                fm_ready = sV.n
                postln_T = sT.n
                cosK, cosQ = tabs[:, 0:T], tabs[:, T:2 * T]
                sinK, sinQ = tabs[:, 2 * T:3 * T], tabs[:, 3 * T:4 * T]

                with ExitStack() as rK:
                    kres = SR("kres", [128, NKC * D], BF16, rK)

                    def g_wk(e):
                        for m in range(NKC):
                            mst.issue(e, mring[mst.ni % 4][:, :],
                                      wk_d[:, D * m:D * (m + 1)])
                    block.gpsimd(g_wk)
                    wk_i = list(range(0, NKC))

                    def g_wv(e, wt=None, wv=None):
                        e.wait_ge(sT.h, postln_T)
                        e.wait_ge(sV.h, fm_ready)
                        for k in range(NKC):
                            kst.issue(e, kres[:, D * k:D * (k + 1)],
                                      wv_d[:, D * k:D * (k + 1)])
                        kst.join(e)
                        e.sem_inc(kresdone.h, 1)
                    block.gpsimd(g_wv)
                    kresdone.n += 1
                    kres_wv = kresdone.n

                    def qk_proj(widx, cosT, sinT, out_cb, contrib, join_s, join_v):
                        ev_snap = [0] * NH
                        rope_snap = [0] * NH
                        ci_snap = [None] * NH
                        for mc in range(NH):
                            pj = pA if mc % 2 == 0 else pB

                            def t_mm(e, m=mc, ps=pj, wi=widx[mc]):
                                mst.wait(e, wi)
                                if m < 2:
                                    e.wait_ge(sV.h, fm_ready)
                                    if join_s:
                                        e.wait_ge(sS.h, join_s)
                                else:
                                    e.wait_ge(sS.h, ev_snap[m - 2])
                                pan = mring[wi % 4]
                                for n in range(2):
                                    for kc in range(NKC):
                                        mm = e.matmul(
                                            ps[:, 512 * n:512 * (n + 1)],
                                            pan[:, 128 * kc:128 * (kc + 1)],
                                            fmA[:, T * kc + 512 * n:T * kc + 512 * (n + 1)],
                                            start=(kc == 0), stop=(kc == NKC - 1))
                                        if kc == NKC - 1:
                                            sT.bump(mm)
                            block.tensor(t_mm)

                            def s_ev(e, m=mc, ps=pj, w=sT.n):
                                e.wait_ge(sT.h, w)
                                e.nop().then_inc(mst.freesem, 1)
                                if m >= 2:
                                    e.wait_ge(sV.h, rope_snap[m - 2])
                                    if contrib:
                                        cst.wait(e, ci_snap[m - 2])
                                sS.bump(e.activation(out=krow[m % 2][:, :],
                                                     in_=ps[:, :],
                                                     func=AF.Copy, scale=1.0))
                            block.scalar(s_ev)
                            ev_snap[mc] = sS.n

                            def t_roll(e, m=mc, w=sS.n):
                                e.wait_ge(sS.h, w)
                                if m >= 1:
                                    e.wait_ge(sV.h, rope_snap[m - 1])
                                elif join_v:
                                    e.wait_ge(sV.h, join_v)
                                for n in range(2):
                                    sT.bump(e.matmul(pC[:, 512 * n:512 * (n + 1)],
                                                     shiftm[:, :],
                                                     krow[m % 2][:, 512 * n:512 * (n + 1)],
                                                     start=True, stop=True))
                            block.tensor(t_roll)

                            def v_rope(e, m=mc, w=sT.n):
                                e.wait_ge(sT.h, w)
                                sV.bump(e.tensor_tensor(out=tt1[:, :],
                                                        in0=krow[m % 2][:, :],
                                                        in1=cosT, op=ALU.mult))
                                sV.bump(e.tensor_tensor(out=tt2[:, :], in0=pC[:, :],
                                                        in1=sinT, op=ALU.mult))
                                e.wait_ge(sV.h, sV.n)
                                sV.bump(out_cb(e, m))
                            block.vector(v_rope)
                            rope_snap[mc] = sV.n

                            if contrib:
                                def y_c(e, m=mc, w=sV.n):
                                    e.wait_ge(sV.h, w)
                                    cst.issue(
                                        e,
                                        kgi.rearrange("(mc p) t -> mc p t", p=128)[m],
                                        krow[m % 2][:, :])
                                block.sync(y_c)
                                ci_snap[mc] = cst.ni - 1
                        return rope_snap

                    def k_out(e, m):
                        return e.tensor_tensor(out=krow[m % 2][:, :], in0=tt1[:, :],
                                               in1=tt2[:, :], op=ALU.add)
                    qk_proj(wk_i, cosK, sinK, k_out, contrib=True,
                            join_s=0, join_v=0)
                    kc_last = cst.ni - 1

                    def g_ccK(e):
                        cst.join(e)
                        e.collective_compute(
                            "AllGather", ALU.bypass, replica_groups=GROUPS,
                            ins=[kgi], outs=[kgo]).then_inc(ccK.h, 1)
                    block.gpsimd(g_ccK)
                    ccK.n += 1

                    vev_snap = [0] * NTC
                    vci = [None] * NTC
                    for tc in range(NTC):
                        def t_vmm(e, t=tc):
                            e.wait_ge(kresdone.h, kres_wv)
                            if t == 0:
                                e.wait_ge(sV.h, fm_ready)
                                e.wait_ge(sS.h, sS.n)
                            else:
                                e.wait_ge(sS.h, vev_snap[t - 1])
                            for n in range(4):
                                ps = pA if n < 2 else pB
                                psl = ps[:, 512 * (n % 2):512 * (n % 2 + 1)]
                                for kc in range(NKC):
                                    mm = e.matmul(
                                        psl,
                                        fmA[:, T * kc + 128 * t:T * kc + 128 * (t + 1)],
                                        kres[:, D * kc + 512 * n:D * kc + 512 * (n + 1)],
                                        start=(kc == 0), stop=(kc == NKC - 1))
                                    if kc == NKC - 1:
                                        sT.bump(mm)
                        block.tensor(t_vmm)

                        def s_vev(e, t=tc, w=sT.n):
                            e.wait_ge(sT.h, w)
                            if t == 0:
                                cst.wait(e, kc_last - 1)
                                cst.wait(e, kc_last)
                            else:
                                cst.wait(e, vci[t - 1][0])
                                cst.wait(e, vci[t - 1][1])
                            sS.bump(e.activation(out=krow[0][:, :], in_=pA[:, :],
                                                 func=AF.Copy, scale=1.0))
                            sS.bump(e.activation(out=krow[1][:, :], in_=pB[:, :],
                                                 func=AF.Copy, scale=1.0))
                        block.scalar(s_vev)
                        vev_snap[tc] = sS.n

                        def y_vc(e, t=tc, w=sS.n):
                            e.wait_ge(sS.h, w)
                            a = cst.issue(
                                e,
                                vgi.rearrange("(h j p) d -> j p h d",
                                              h=NH, p=128)[t][:, 0:8, :],
                                krow[0][:, :].rearrange("p (h d) -> p h d", d=128))
                            b = cst.issue(
                                e,
                                vgi.rearrange("(h j p) d -> j p h d",
                                              h=NH, p=128)[t][:, 8:16, :],
                                krow[1][:, :].rearrange("p (h d) -> p h d", d=128))
                            vci[t] = (a, b)
                        block.sync(y_vc)

                    def g_ccV(e):
                        cst.join(e)
                        e.collective_compute(
                            "AllGather", ALU.bypass, replica_groups=GROUPS,
                            ins=[vgi], outs=[vgo]).then_inc(ccV.h, 1)
                    block.gpsimd(g_ccV)
                    ccV.n += 1
                    vproj_S = sS.n
                    vc_last = cst.ni - 1

                    def g_wq(e):
                        for m in range(NKC):
                            mst.issue(e, mring[mst.ni % 4][:, :],
                                      wq_d[:, D * m:D * (m + 1)])
                    block.gpsimd(g_wq)
                    wq_i = list(range(NKC, 2 * NKC))

                    def q_out(e, m):
                        return e.tensor_tensor(out=qall[:, T * m:T * (m + 1)],
                                               in0=tt1[:, :], in1=tt2[:, :],
                                               op=ALU.add)
                    qvj = sS.n

                    def s_qgate(e):
                        cst.wait(e, vc_last - 1)
                        cst.wait(e, vc_last)
                        sS.bump(e.nop())
                    block.scalar(s_qgate)
                    qk_proj(wq_i, cosQ, sinQ, q_out, contrib=False,
                            join_s=vproj_S, join_v=sV.n)
                    q_ready = sV.n
                    q_sS = sS.n
                qproj_T = sT.n
            with ExitStack() as rAT:
                kbuf = [SR(f"kb{i}", [128, SEQ], BF16, rAT) for i in range(2)]
                vbuf = [SR(f"vb{i}", [128, 32 * 129], BF16, rAT) for i in range(2)]
                ptile = [SR(f"pt{i}", [128, T], BF16, rAT) for i in range(2)]
                ctxrow = [SR(f"cr{i}", [128, T], BF16, rAT) for i in range(2)]

                def v_ones(e):
                    for i in range(2):
                        sV.bump(e.memset(
                            vbuf[i][:, :].rearrange("p (g c) -> p g c", c=129)
                            [:, :, 128:129], 1.0))
                block.vector(v_ones)
                ones_v = sV.n

                NIT = NH * NTC
                sgrp = [0] * (4 * NIT)
                pvgrp = [0] * (4 * NIT)
                expgrp = [0] * (4 * NIT)
                ctx_ev = [0] * NIT
                hctr = [0] * NH
                cev_snap = [0] * NH

                for h in range(NH):
                    def y_kv(e, hh=h):
                        if hh == 0:
                            e.wait_ge(ccK.h, 1)
                            e.wait_ge(ccV.h, 1)
                        if hh < 2:
                            e.wait_ge(sT.h, qproj_T)
                            e.wait_ge(sV.h, q_ready)
                            e.wait_ge(sS.h, q_sS)
                        else:
                            e.wait_ge(kvf.h, hh - 1)
                        base = kvst.ni
                        for r in range(NRANK):
                            kvst.issue(
                                e, kbuf[hh % 2][:, T * r:T * (r + 1)],
                                kgo.rearrange("(r mc p) t -> r mc p t",
                                              mc=NKC, p=128)[r, hh])
                        for r in range(NRANK):
                            kvst.issue(
                                e,
                                vbuf[hh % 2][:, :].rearrange("p (g c) -> p g c", c=129)
                                [:, 8 * r:8 * (r + 1), 0:128],
                                vgo.rearrange("(r h j p) d -> r h p j d",
                                              h=NH, j=8, p=128)[r, hh])
                        for j in range(8):
                            kvst.wait(e, base + j)
                        e.sem_inc(kvdone.h, 1)
                    block.sync(y_kv)
                    kvdone.n += 1

                attn_join_s = sS.n
                for h in range(NH):
                    for qt in range(NTC):
                        it = h * NTC + qt
                        exp_base = sS.n
                        for g in range(4):
                            expgrp[4 * it + g] = exp_base + g + 1

                        def t_attn(e, hh=h, q=qt, i=it):
                            if i == 0:
                                e.wait_ge(sV.h, max(q_ready, ones_v))
                                e.wait_ge(sS.h, attn_join_s)
                            if q == 0:
                                e.wait_ge(kvdone.h, hh + 1)
                            for g in range(4):
                                gg = 4 * i + g
                                ps = pA if gg % 2 == 0 else pB
                                if gg >= 2:
                                    e.wait_ge(sS.h, expgrp[gg - 2])
                                for s_ in range(8):
                                    kt = 8 * g + s_
                                    sT.bump(e.matmul(
                                        ps[:, 128 * s_:128 * (s_ + 1)],
                                        kbuf[hh % 2][:, 128 * kt:128 * (kt + 1)],
                                        qall[:, T * hh + 128 * q:T * hh + 128 * (q + 1)],
                                        start=True, stop=True))
                                sgrp[gg] = sT.n
                            pctx = pF[:, 0:129] if i % 2 == 0 else pC[:, 0:129]
                            for g in range(4):
                                gg = 4 * i + g
                                e.wait_ge(sS.h, expgrp[gg])
                                if g == 0 and i >= 2:
                                    e.wait_ge(sV.h, ctx_ev[i - 2])
                                for s_ in range(8):
                                    kt = 8 * g + s_
                                    sT.bump(e.matmul(
                                        pctx,
                                        ptile[gg % 2][:, 128 * s_:128 * (s_ + 1)],
                                        vbuf[hh % 2][:, 129 * kt:129 * kt + 129],
                                        start=(g == 0 and s_ == 0),
                                        stop=(g == 3 and s_ == 7)))
                                pvgrp[gg] = sT.n
                        block.tensor(t_attn)

                        def s_exp(e, i=it):
                            for g in range(4):
                                gg = 4 * i + g
                                ps = pA if gg % 2 == 0 else pB
                                e.wait_ge(sT.h, sgrp[gg])
                                if gg >= 2:
                                    e.wait_ge(sT.h, pvgrp[gg - 2])
                                sS.bump(e.activation(out=ptile[gg % 2][:, :],
                                                     in_=ps[:, :], func=AF.Exp,
                                                     bias=consts[:, 1:2], scale=1.0))
                        block.scalar(s_exp)
                        assert sS.n == exp_base + 4

                        def v_ctx(e, hh=h, q=qt, i=it, w=sT.n,
                                  wtr=(hctr[h - 2] if (h >= 2 and qt == 0) else 0)):
                            e.wait_ge(sT.h, w)
                            if q == NTC - 1:
                                e.nop().then_inc(kvf.h, 1)
                            if wtr:
                                e.wait_ge(sT.h, wtr)
                            pctx = pF[:, 0:129] if i % 2 == 0 else pC[:, 0:129]
                            e.wait_ge(sV.h, sV.n)
                            sV.bump(e.reciprocal(recl[:, :], pctx[:, 128:129]))
                            e.wait_ge(sV.h, sV.n)
                            sV.bump(e.tensor_scalar(
                                out=ctxrow[hh % 2][:, 128 * q:128 * (q + 1)],
                                in0=pctx[:, 0:128], scalar1=recl[:, 0:1],
                                scalar2=None, op0=ALU.mult))
                        block.vector(v_ctx)
                        ctx_ev[it] = sV.n
                    def t_ctr(e, hh=h, w=sV.n,
                              wev=(cev_snap[h - 1] if h >= 1 else 0)):
                        e.wait_ge(sV.h, w)
                        if wev:
                            e.wait_ge(sV.h, wev)
                        for q2 in range(NTC):
                            sT.bump(e.transpose(
                                pE[:, 128 * q2:128 * (q2 + 1)],
                                ctxrow[hh % 2][:, 128 * q2:128 * (q2 + 1)],
                                ident[:, :]))
                    block.tensor(t_ctr)
                    hctr[h] = sT.n

                    def v_cev(e, hh=h, w=sT.n):
                        e.wait_ge(sT.h, w)
                        sV.bump(e.tensor_copy(out=cfm[:, T * hh:T * (hh + 1)],
                                              in_=pE[:, :]))
                    block.vector(v_cev)
                    cev_snap[h] = sV.n
                attn_T = sT.n

            with ExitStack() as rO:
                kres = SR("kreso", [128, NKC * D], BF16, rO)
                xinb = [SR(f"xia{i}", [128, D], F32, rO) for i in range(2)]
                xoutb = [SR(f"xoa{i}", [128, D], F32, rO) for i in range(2)]

                ctx_done = sV.n

                def g_wo(e):
                    e.wait_ge(sT.h, attn_T)
                    e.wait_ge(sV.h, ctx_done)
                    for k in range(NKC):
                        kst.issue(e, kres[:, D * k:D * (k + 1)],
                                  wo_d[:, D * k:D * (k + 1)])
                    kst.join(e)
                    e.sem_inc(kresdone.h, 1)
                block.gpsimd(g_wo)
                kresdone.n += 1
                kres_wo = kresdone.n

                def o_mm(e, t):
                    if t == 0:
                        e.wait_ge(kresdone.h, kres_wo)
                        e.wait_ge(sV.h, ctx_done)
                        e.wait_ge(sS.h, sS.n)
                    for n in range(4):
                        ps = pA if n < 2 else pB
                        psl = ps[:, 512 * (n % 2):512 * (n % 2 + 1)]
                        for kc in range(NKC):
                            mm = e.matmul(
                                psl,
                                cfm[:, T * kc + 128 * t:T * kc + 128 * (t + 1)],
                                kres[:, D * kc + 512 * n:D * kc + 512 * (n + 1)],
                                start=(kc == 0), stop=(kc == NKC - 1))
                            if kc == NKC - 1:
                                sT.bump(mm)

                def o_ev(e, t):
                    sV.bump(e.tensor_tensor(out=xoutb[t % 2][:, 0:1024], in0=pA[:, :],
                                            in1=xinb[t % 2][:, 0:1024], op=ALU.add))
                    sV.bump(e.tensor_tensor(out=xoutb[t % 2][:, 1024:2048],
                                            in0=pB[:, :],
                                            in1=xinb[t % 2][:, 1024:2048],
                                            op=ALU.add))
                resid_tm(xrows, xdrows, xinb, xoutb, o_mm, o_ev,
                         gate_t=attn_T, gate_v=ctx_done)

        def ffn_sublayer(tag, src_rows, dst_rows):
            with ExitStack() as ph:
                fmA = SR(f"fmA{tag}", [128, NKC * T], BF16, ph)
                entry_T = sT.n
                with ExitStack() as rB:
                    lnb = [SR(f"lnb{tag}{i}", [128, D], F32, rB) for i in range(2)]
                    lnbb = [SR(f"lnbb{tag}{i}", [128, D], BF16, rB) for i in range(2)]
                    ln_phase(src_rows, fmA, lnb, lnbb)
                fm_ready = sV.n

                hdn = SR(f"hdn{tag}", [128, NIC * T], BF16, ph)
                silu_t = SR(f"sil{tag}", [128, T], F32, ph)
                usc_t = SR(f"usc{tag}", [128, T], F32, ph)
                dnring = [SR(f"dnr{tag}{i}", [128, INTER], BF16, ph) for i in range(3)]
                dstile = [SR(f"dt{tag}{i}", [128, T], BF16, ph) for i in range(2)]
                stage = [SR(f"stg{tag}{i}", [128, T], F32, ph) for i in range(2)]

                gu_i = []

                def g_gu(e):
                    for ic in range(NIC):
                        gu_i.append(mst.issue(e, mring[mst.ni % 4][:, :],
                                              gate_d[:, D * ic:D * (ic + 1)]))
                        gu_i.append(mst.issue(e, mring[mst.ni % 4][:, :],
                                              up_d[:, D * ic:D * (ic + 1)]))
                block.gpsimd(g_gu)

                silu_snap = [0] * NIC
                usc_snap = [0] * NIC
                mult_snap = [0] * NIC
                join_s0 = sS.n
                for ic in range(NIC):
                    def t_g(e, i=ic, wi=gu_i[2 * ic]):
                        mst.wait(e, wi)
                        if i == 0:
                            e.wait_ge(sV.h, fm_ready)
                            e.wait_ge(sS.h, join_s0)
                        else:
                            e.wait_ge(sS.h, silu_snap[i - 1])
                        pan = mring[wi % 4]
                        for n in range(2):
                            for kc in range(NKC):
                                mm = e.matmul(
                                    pA[:, 512 * n:512 * (n + 1)],
                                    pan[:, 128 * kc:128 * (kc + 1)],
                                    fmA[:, T * kc + 512 * n:T * kc + 512 * (n + 1)],
                                    start=(kc == 0), stop=(kc == NKC - 1))
                                if kc == NKC - 1:
                                    sT.bump(mm)
                    block.tensor(t_g)
                    tg = sT.n

                    def t_u(e, i=ic, wi=gu_i[2 * ic + 1]):
                        mst.wait(e, wi)
                        if i >= 1:
                            e.wait_ge(sS.h, usc_snap[i - 1])
                        pan = mring[wi % 4]
                        for n in range(2):
                            for kc in range(NKC):
                                mm = e.matmul(
                                    pB[:, 512 * n:512 * (n + 1)],
                                    pan[:, 128 * kc:128 * (kc + 1)],
                                    fmA[:, T * kc + 512 * n:T * kc + 512 * (n + 1)],
                                    start=(kc == 0), stop=(kc == NKC - 1))
                                if kc == NKC - 1:
                                    sT.bump(mm)
                    block.tensor(t_u)
                    tu = sT.n

                    def s_silu(e, i=ic, wg=tg, wu=tu):
                        e.wait_ge(sT.h, wg)
                        e.nop().then_inc(mst.freesem, 1)
                        if i >= 1:
                            e.wait_ge(sV.h, mult_snap[i - 1])
                        sS.bump(e.activation(out=silu_t[:, :], in_=pA[:, :],
                                             func=AF.Silu, scale=gs_t[:, i:i + 1]))
                        silu_snap[i] = sS.n
                        e.wait_ge(sT.h, wu)
                        e.nop().then_inc(mst.freesem, 1)
                        sS.bump(e.activation(out=usc_t[:, :], in_=pB[:, :],
                                             func=AF.Copy, scale=us_t[:, i:i + 1]))
                        usc_snap[i] = sS.n
                    block.scalar(s_silu)

                    def v_hdn(e, i=ic):
                        e.wait_ge(sS.h, usc_snap[i])
                        sV.bump(e.tensor_tensor(out=hdn[:, T * i:T * (i + 1)],
                                                in0=silu_t[:, :], in1=usc_t[:, :],
                                                op=ALU.mult))
                    block.vector(v_hdn)
                    mult_snap[ic] = sV.n
                hdn_ready = sV.n

                dn_i = []
                postln_Tf = sT.n

                def g_dn(e, w=None):
                    e.wait_ge(sT.h, postln_Tf)
                    ost.join(e)
                    for mc in range(NKC):
                        dn_i.append(dstm.issue(e, dnring[dstm.ni % 3][:, :],
                                               down_d[:, INTER * mc:INTER * (mc + 1)]))
                block.gpsimd(g_dn)

                dsev_snap = [0] * NKC
                add_snap = [0] * NKC
                wb_idx = [None] * NKC
                join_s1 = sS.n
                for mc in range(NKC):
                    def t_d(e, m=mc, wi=dn_i[mc]):
                        dstm.wait(e, wi)
                        if m == 0:
                            e.wait_ge(sV.h, hdn_ready)
                            e.wait_ge(sS.h, join_s1)
                        elif m >= 2:
                            e.wait_ge(sS.h, dsev_snap[m - 2])
                        ps = pA if m % 2 == 0 else pB
                        pan = dnring[wi % 3]
                        for n in range(2):
                            for kc in range(NIC):
                                mm = e.matmul(
                                    ps[:, 512 * n:512 * (n + 1)],
                                    pan[:, 128 * kc:128 * (kc + 1)],
                                    hdn[:, T * kc + 512 * n:T * kc + 512 * (n + 1)],
                                    start=(kc == 0), stop=(kc == NIC - 1))
                                if kc == NIC - 1:
                                    sT.bump(mm)
                    block.tensor(t_d)
                    td = sT.n

                    def s_ds(e, m=mc, w=td):
                        e.wait_ge(sT.h, w)
                        e.nop().then_inc(dstm.freesem, 1)
                        if m >= 2:
                            e.wait_ge(sV.h, add_snap[m - 2])
                        ps = pA if m % 2 == 0 else pB
                        sS.bump(e.activation(out=dstile[m % 2][:, :], in_=ps[:, :],
                                             func=AF.Copy, scale=ds_t[:, m:m + 1]))
                    block.scalar(s_ds)
                    dsev_snap[mc] = sS.n

                    def y_rmwl(e, m=mc,
                               wwb=(wb_idx[mc - 2] if mc >= 2 else None)):
                        if wwb is not None:
                            ost.wait(e, wwb)
                        xst.issue(
                            e,
                            stage[m % 2][:, :].rearrange("p (j q) -> p j q", q=128),
                            src_rows.transpose([1, 0, 2])[:, :, 128 * m:128 * (m + 1)])
                    block.sync(y_rmwl)
                    li = xst.ni - 1

                    def t_tr(e, m=mc, w=dsev_snap[mc]):
                        e.wait_ge(sS.h, w)
                        if m >= 1:
                            e.wait_ge(sV.h, add_snap[m - 1])
                        for j in range(8):
                            sT.bump(e.transpose(
                                pE[:, 128 * j:128 * (j + 1)],
                                dstile[m % 2][:, 128 * j:128 * (j + 1)],
                                ident[:, :]))
                    block.tensor(t_tr)

                    def v_add(e, m=mc, w=sT.n, i=li,
                              wwb=(wb_idx[mc - 2] if mc >= 2 else None)):
                        e.wait_ge(sT.h, w)
                        xst.wait(e, i)
                        if wwb is not None:
                            ost.wait(e, wwb)
                        sV.bump(e.tensor_tensor(out=stage[m % 2][:, :], in0=pE[:, :],
                                                in1=stage[m % 2][:, :], op=ALU.add))
                        e.wait_ge(sV.h, sV.n)
                        e.sem_inc(xst.freesem, 1)
                    block.vector(v_add)
                    add_snap[mc] = sV.n

                    def y_wb(e, m=mc, w=sV.n):
                        e.wait_ge(sV.h, w)
                        wb_idx[m] = ost.issue(
                            e,
                            dst_rows.transpose([1, 0, 2])[:, :, 128 * m:128 * (m + 1)],
                            stage[m % 2][:, :].rearrange("p (j q) -> p j q", q=128))
                    block.sync(y_wb)

        ffn_sublayer(1, xdrows, xdrows)

        with ExitStack() as ph:
            fmA = SR("fmA2", [128, NKC * T], BF16, ph)
            entry_T = sT.n
            with ExitStack() as rB:
                lnb = [SR(f"lnb2{i}", [128, D], F32, rB) for i in range(2)]
                lnbb = [SR(f"lnbb2{i}", [128, D], BF16, rB) for i in range(2)]
                ln_phase(xdrows, fmA, lnb, lnbb)
            fm_ready = sV.n
            postln_T2 = sT.n

            kres = SR("kres2", [128, NKC * D], BF16, ph)

            def g_bk(e, w=None):
                e.wait_ge(sT.h, postln_T2)
                e.wait_ge(sV.h, fm_ready)
                for k in range(NKC):
                    kst.issue(e, kres[:, D * k:D * (k + 1)], bk_d[:, D * k:D * (k + 1)])
                kst.join(e)
                e.sem_inc(kresdone.h, 1)
            block.gpsimd(g_bk)
            kresdone.n += 1
            kres_bk = kresdone.n

            with ExitStack() as rKV:
                ktm = SR("ktm", [128, NTC * D], BF16, rKV)
                vtm = SR("vtm", [128, NTC * BH * 33], BF16, rKV)
                vkts = SR("vkts", [32, BH * 33], F32, rKV)

                def v_ones2(e, w=entry_T):
                    e.wait_ge(sT.h, w)
                    sV.bump(e.memset(
                        vtm[:, :].rearrange("p (t h c) -> p t h c", h=BH, c=33)
                        [:, :, :, 32:33], 1.0))
                block.vector(v_ones2)
                ones2_v = sV.n

                def tm_proj_b(batch_ready, dest_k, evsnap, join_s):
                    for tc in range(NTC):
                        def t_mm(e, t=tc):
                            e.wait_ge(kresdone.h, batch_ready)
                            if t == 0:
                                e.wait_ge(sV.h, max(fm_ready, ones2_v))
                                e.wait_ge(sS.h, join_s)
                            else:
                                e.wait_ge(sS.h, evsnap[t - 1])
                            for n in range(4):
                                ps = pA if n < 2 else pB
                                psl = ps[:, 512 * (n % 2):512 * (n % 2 + 1)]
                                for kc in range(NKC):
                                    mm = e.matmul(
                                        psl,
                                        fmA[:, T * kc + 128 * t:T * kc + 128 * (t + 1)],
                                        kres[:, D * kc + 512 * n:D * kc + 512 * (n + 1)],
                                        start=(kc == 0), stop=(kc == NKC - 1))
                                    if kc == NKC - 1:
                                        sT.bump(mm)
                        block.tensor(t_mm)

                        def s_ev(e, t=tc, w=sT.n):
                            e.wait_ge(sT.h, w)
                            if dest_k:
                                sS.bump(e.activation(out=ktm[:, D * t:D * t + 1024],
                                                     in_=pA[:, :], func=AF.Relu,
                                                     scale=1.0))
                                sS.bump(e.activation(
                                    out=ktm[:, D * t + 1024:D * (t + 1)],
                                    in_=pB[:, :], func=AF.Relu, scale=1.0))
                            else:
                                vv = vtm[:, 2112 * t:2112 * (t + 1)].rearrange(
                                    "p (h c) -> p h c", c=33)
                                sS.bump(e.activation(out=vv[:, 0:32, 0:32],
                                                     in_=pA[:, :].rearrange(
                                                         "p (h d) -> p h d", d=32),
                                                     func=AF.Copy, scale=1.0))
                                sS.bump(e.activation(
                                    out=vv[:, 32:64, 0:32],
                                    in_=pB[:, :].rearrange("p (h d) -> p h d", d=32),
                                    func=AF.Copy, scale=1.0))
                            evsnap[t] = sS.n
                        block.scalar(s_ev)

                kev = [0] * NTC
                tm_proj_b(kres_bk, True, kev, join_s=sS.n)
                kproj_T = sT.n

                def g_bv(e):
                    e.wait_ge(sT.h, kproj_T)
                    for k in range(NKC):
                        kst.issue(e, kres[:, D * k:D * (k + 1)],
                                  bv_d[:, D * k:D * (k + 1)])
                    kst.join(e)
                    e.sem_inc(kresdone.h, 1)
                block.gpsimd(g_bv)
                kresdone.n += 1

                vev = [0] * NTC
                tm_proj_b(kresdone.n, False, vev, join_s=sS.n)
                vproj_T = sT.n
                kv_ready = sS.n

                def g_bp(e):
                    e.wait_ge(sT.h, vproj_T)
                    for k in range(NKC):
                        kst.issue(e, kres[:, D * k:D * (k + 1)],
                                  bp_d[:, D * k:D * (k + 1)])
                    kst.join(e)
                    e.sem_inc(kresdone.h, 1)
                block.gpsimd(g_bp)
                kresdone.n += 1
                kres_bp = kresdone.n

                HB = 8
                vkev = [0] * (BH // HB)
                for hb in range(BH // HB):
                    def t_vk(e, g=hb):
                        if g == 0:
                            e.wait_ge(sS.h, kv_ready)
                        elif g >= 2:
                            e.wait_ge(sV.h, vkev[g - 2])
                        pvk = pF if g % 2 == 0 else pC
                        for j in range(HB):
                            hh = g * HB + j
                            for t in range(NTC):
                                mm = e.matmul(
                                    pvk[0:32, 33 * j:33 * (j + 1)],
                                    ktm[:, D * t + 32 * hh:D * t + 32 * (hh + 1)],
                                    vtm[:, 2112 * t + 33 * hh:2112 * t + 33 * (hh + 1)],
                                    start=(t == 0), stop=(t == NTC - 1))
                                if t == NTC - 1:
                                    sT.bump(mm)
                    block.tensor(t_vk)

                    def v_vkev(e, g=hb, w=sT.n):
                        e.wait_ge(sT.h, w)
                        pvk = pF if g % 2 == 0 else pC
                        sV.bump(e.tensor_copy(
                            out=vkts[:, 33 * HB * g:33 * HB * (g + 1)],
                            in_=pvk[0:32, 0:33 * HB]))
                    block.vector(v_vkev)
                    vkev[hb] = sV.n

                def y_vkout(e, w=sV.n):
                    e.wait_ge(sV.h, w)
                    i = yst.issue(e, vki, vkts[:, :])
                    yst.wait(e, i)
                    e.sem_inc(ydone.h, 1)
                block.sync(y_vkout)
                ydone.n += 1

                def g_ccR(e, w=ydone.n):
                    e.wait_ge(ydone.h, w)
                    e.collective_compute("AllReduce", ALU.add, replica_groups=GROUPS,
                                         ins=[vki], outs=[vko]).then_inc(ccR.h, 1)
                block.gpsimd(g_ccR)
                ccR.n += 1
                vk_T = sT.n

            with ExitStack() as rQ:
                fmB = SR("fmB", [128, NKC * T], BF16, rQ)
                vkt4 = SR("vkt4", [128, BH * 33], BF16, rQ)
                obsb = [SR(f"ob{i}", [32, T], F32, rQ) for i in range(2)]
                dden = [SR(f"dden{i}", [33, T], F32, rQ) for i in range(2)]
                drecb = [SR(f"drecb{i}", [33, T], BF16, rQ) for i in range(2)]

                def v_qzero(e):
                    e.wait_ge(ccR.h, 1)
                    e.wait_ge(sT.h, vk_T)
                    sV.bump(e.memset(vkt4[:, :], 0.0))
                    sV.bump(e.memset(drecb[0][:, :], 0.0))
                    sV.bump(e.memset(drecb[1][:, :], 0.0))
                block.vector(v_qzero)
                qzero_v = sV.n

                def g_vkt4(e):
                    e.wait_ge(ccR.h, 1)
                    e.wait_ge(sT.h, vk_T)
                    e.wait_ge(sV.h, qzero_v)
                    base = kst.ni
                    for j in range(4):
                        kst.issue(e, vkt4[32 * j:32 * (j + 1), :].rearrange(
                            "p (b r c) -> p b r c", r=4, c=33)[:, :, j, :],
                            vko.rearrange("p (b j c) -> p b j c", j=4, c=33)[:, :, j, :])
                    for j in range(4):
                        kst.wait(e, base + j)
                    e.sem_inc(ydone.h, 1)
                block.gpsimd(g_vkt4)
                ydone.n += 1
                vkt4_ready = ydone.n

                bq_i = []

                def g_bq(e):
                    for m in range(NKC):
                        bq_i.append(mst.issue(e, mring[mst.ni % 4][:, :],
                                              bq_d[:, D * m:D * (m + 1)]))
                block.gpsimd(g_bq)

                qrelu = [0] * NKC
                join_q = sS.n
                for mc in range(NKC):
                    def t_q(e, m=mc, wi=bq_i[mc]):
                        mst.wait(e, wi)
                        if m == 0:
                            e.wait_ge(sV.h, fm_ready)
                            e.wait_ge(sS.h, join_q)
                        elif m >= 2:
                            e.wait_ge(sS.h, qrelu[m - 2])
                        ps = pA if m % 2 == 0 else pB
                        pan = mring[wi % 4]
                        for n in range(2):
                            for kc in range(NKC):
                                mm = e.matmul(
                                    ps[:, 512 * n:512 * (n + 1)],
                                    pan[:, 128 * kc:128 * (kc + 1)],
                                    fmA[:, T * kc + 512 * n:T * kc + 512 * (n + 1)],
                                    start=(kc == 0), stop=(kc == NKC - 1))
                                if kc == NKC - 1:
                                    sT.bump(mm)
                    block.tensor(t_q)

                    def s_q(e, m=mc, w=sT.n, wg=vk_T):
                        e.wait_ge(sT.h, max(w, wg))
                        e.nop().then_inc(mst.freesem, 1)
                        ps = pA if m % 2 == 0 else pB
                        sS.bump(e.activation(out=fmB[:, T * m:T * (m + 1)],
                                             in_=ps[:, :], func=AF.Relu, scale=1.0))
                    block.scalar(s_q)
                    qrelu[mc] = sS.n
                q_ready2 = sS.n

                outbuf = fmA
                ob_ev = [0] * BH
                ob_mult = [0] * BH
                den_snap = [0] * BH
                for h in range(BH):
                    def t_ob(e, hh=h):
                        if hh == 0:
                            e.wait_ge(ydone.h, vkt4_ready)
                            e.wait_ge(sS.h, q_ready2)
                        elif hh >= 2:
                            e.wait_ge(sS.h, ob_ev[hh - 2])
                            e.wait_ge(sV.h, den_snap[hh - 2])
                        ps = pA if hh % 2 == 0 else pB
                        r, b = hh % 4, hh // 4
                        for n in range(2):
                            sT.bump(e.matmul(
                                ps[0:33, 512 * n:512 * (n + 1)],
                                vkt4[:, 33 * hh:33 * (hh + 1)],
                                fmB[:, T * b + 512 * n:T * b + 512 * (n + 1)],
                                start=True, stop=True))
                    block.tensor(t_ob)
                    tob = sT.n

                    def v_den(e, hh=h, w=tob):
                        e.wait_ge(sT.h, w)
                        ps = pA if hh % 2 == 0 else pB
                        dd, dr = dden[hh % 2], drecb[hh % 2]
                        sV.bump(e.tensor_scalar(out=dd[32:33, :],
                                                in0=ps[32:33, 0:1024],
                                                scalar1=1e-15, scalar2=1e-12,
                                                op0=ALU.add, op1=ALU.max))
                        e.wait_ge(sV.h, sV.n)
                        sV.bump(e.reciprocal(dd[32:33, :], dd[32:33, :]))
                        e.wait_ge(sV.h, sV.n)
                        sV.bump(e.tensor_copy(out=dr[32:33, :], in_=dd[32:33, :]))
                    block.vector(v_den)
                    den_snap[h] = sV.n

                    def t_bc(e, hh=h, w=sV.n):
                        e.wait_ge(sV.h, w)
                        if hh >= 2:
                            e.wait_ge(sV.h, ob_mult[hh - 2])
                        pb0 = 0 if hh % 2 == 0 else 64
                        for n in range(2):
                            sT.bump(e.matmul(pC[pb0:pb0 + 32, 512 * n:512 * (n + 1)],
                                             ones33[:, :],
                                             drecb[hh % 2][0:33, 512 * n:512 * (n + 1)],
                                             start=True, stop=True))
                    block.tensor(t_bc)
                    tbc = sT.n

                    def s_ob(e, hh=h, w=tob):
                        e.wait_ge(sT.h, w)
                        if hh >= 2:
                            e.wait_ge(sV.h, ob_mult[hh - 2])
                        ps = pA if hh % 2 == 0 else pB
                        sS.bump(e.activation(out=obsb[hh % 2][:, :],
                                             in_=ps[0:32, 0:1024],
                                             func=AF.Copy, scale=1.0))
                    block.scalar(s_ob)
                    ob_ev[h] = sS.n

                    def v_obm(e, hh=h, w=sS.n, wb=tbc):
                        e.wait_ge(sS.h, w)
                        e.wait_ge(sT.h, wb)
                        pb0 = 0 if hh % 2 == 0 else 64
                        r, b = hh % 4, hh // 4
                        sV.bump(e.tensor_tensor(
                            out=outbuf[32 * r:32 * (r + 1), T * b:T * (b + 1)],
                            in0=obsb[hh % 2][:, :], in1=pC[pb0:pb0 + 32, :],
                            op=ALU.mult))
                    block.vector(v_obm)
                    ob_mult[h] = sV.n
                outb_ready = sV.n
                oe_T = sT.n

            with ExitStack() as rP:
                bpb_t = SR("bpb_t", [128, D], F32, rP)
                xinb = [SR(f"xib{i}", [128, D], F32, rP) for i in range(2)]
                xoutb = [SR(f"xob{i}", [128, D], F32, rP) for i in range(2)]

                def y_bpb(e):
                    e.wait_ge(sT.h, oe_T)
                    e.wait_ge(sV.h, outb_ready)
                    i = yst.issue(e, bpb_t[:, :], bpb_d)
                    yst.wait(e, i)
                    e.sem_inc(ydone.h, 1)
                block.sync(y_bpb)
                ydone.n += 1
                bpb_ready = ydone.n

                def p_mm(e, t):
                    if t == 0:
                        e.wait_ge(kresdone.h, kres_bp)
                        e.wait_ge(sV.h, outb_ready)
                        e.wait_ge(sS.h, sS.n)
                    for n in range(4):
                        ps = pA if n < 2 else pB
                        psl = ps[:, 512 * (n % 2):512 * (n % 2 + 1)]
                        for kc in range(NKC):
                            mm = e.matmul(
                                psl,
                                outbuf[:, T * kc + 128 * t:T * kc + 128 * (t + 1)],
                                kres[:, D * kc + 512 * n:D * kc + 512 * (n + 1)],
                                start=(kc == 0), stop=(kc == NKC - 1))
                            if kc == NKC - 1:
                                sT.bump(mm)

                def p_ev(e, t):
                    e.wait_ge(ydone.h, bpb_ready)
                    for half, ps in ((0, pA), (1, pB)):
                        sl = slice(1024 * half, 1024 * (half + 1))
                        sV.bump(e.tensor_tensor(out=tt1[:, :], in0=ps[:, :],
                                                in1=bpb_t[:, sl], op=ALU.add))
                        e.wait_ge(sV.h, sV.n)
                        sV.bump(e.tensor_tensor(out=xoutb[t % 2][:, sl], in0=tt1[:, :],
                                                in1=xinb[t % 2][:, sl], op=ALU.add))
                        e.wait_ge(sV.h, sV.n)
                resid_tm(xdrows, xdrows, xinb, xoutb, p_mm, p_ev,
                         gate_t=oe_T, gate_v=outb_ready)

        ffn_sublayer(3, xdrows, orows)

        def y_final(e):
            ost.join(e)
        block.sync(y_final)

    return nc


def _prep_weights(inputs):
    wq = _mpan(np.ascontiguousarray(inputs["L0_attn_a_Wq"]))
    wk = _mpan(np.ascontiguousarray(inputs["L0_attn_a_Wk"]))
    wv = _kpan(np.ascontiguousarray(inputs["L0_attn_a_Wv"]))
    wo = _kpan(np.ascontiguousarray(inputs["L0_attn_a_Wo"]))
    gate = _mpan(np.ascontiguousarray(np.asarray(inputs["L0_ffn_gate_w"]).T))
    up = _mpan(np.ascontiguousarray(np.asarray(inputs["L0_ffn_up_w"]).T))
    down = _mpan(np.ascontiguousarray(np.asarray(inputs["L0_ffn_down_w"]).T))
    wqkv = np.asarray(inputs["L0_attn_b_w_qkv"])
    bq = _mpan(np.ascontiguousarray(wqkv[0:D].T))
    bk = _kpan(np.ascontiguousarray(wqkv[D:2 * D].T))
    bv = _kpan(np.ascontiguousarray(wqkv[2 * D:3 * D].T))
    bp = _kpan(np.ascontiguousarray(np.asarray(inputs["L0_attn_b_w_proj"]).T))
    gs = np.ascontiguousarray(np.asarray(inputs["L0_ffn_gate_s"]).reshape(NIC, 128).T)
    us = np.ascontiguousarray(np.asarray(inputs["L0_ffn_up_s"]).reshape(NIC, 128).T)
    ds = np.ascontiguousarray(np.asarray(inputs["L0_ffn_down_s"]).reshape(NKC, 128).T)
    bpb = np.ascontiguousarray(
        np.broadcast_to(np.asarray(inputs["L0_attn_b_b_proj"])[None, :], (128, D)))
    return dict(wq=wq, wk=wk, wv=wv, wo=wo, gate=gate, up=up, down=down,
                bq=bq, bk=bk, bv=bv, bp=bp, gs=gs, us=us, ds=ds, bpb=bpb)


def kernel(**inputs):
    shared = _prep_weights(inputs)
    shared = {k: np.asarray(v, dtype=np.float32) for k, v in shared.items()}
    xfull = np.asarray(inputs["L0_attn_a_hidden_states"], dtype=np.float32)
    nc = build_nc()
    in_maps = []
    for c in range(8):
        b, s0 = c // 4, (c % 4) * T
        m = dict(shared)
        m["x"] = np.ascontiguousarray(xfull[b, s0:s0 + T])
        m["poff"] = np.full((128, 1), float(s0), np.float32)
        in_maps.append(m)
    res = run_bass_kernel_spmd(nc, in_maps, core_ids=list(range(8)))
    out = np.empty((2, SEQ, D), np.float32)
    for c in range(8):
        b, s0 = c // 4, (c % 4) * T
        out[b, s0:s0 + T] = res.results[c]["out"]
    return out
